# version 08
# speedup vs baseline: 2.4727x; 1.0099x over previous
import math
import numpy as np
import ml_dtypes
import concourse.bass as bass
import concourse.mybir as mybir
from concourse.bass_utils import run_bass_kernel_spmd
from concourse.alu_op_type import AluOpType as ALU

F32 = mybir.dt.float32
BF16 = mybir.dt.bfloat16
I32 = mybir.dt.int32
AF = mybir.ActivationFunctionType
AX = mybir.AxisListType.X

T = 2048
D = 1024
NT = T // 128
ND = D // 128
LN_EPS = 1e-5

ENGINES = ("tensor", "vector", "scalar", "gpsimd", "sync")

STRICT = False

ARENA_BYTES = 207 * 1024


class Ev:
    __slots__ = ("sem", "val")

    def __init__(self, sem, val):
        self.sem = sem
        self.val = val


class Prog:

    def __init__(self, name="prog", strict=True):
        self.strict = strict
        self.nc = bass.Bass("TRN2", target_bir_lowering=False, name=name)
        self.phases = [{e: [] for e in ENGINES}]
        self.sems = {}
        self.cnt = {}
        self.waited = {e: {} for e in ENGINES}
        for e in ENGINES:
            self.sems[e] = self.nc.alloc_semaphore("sem_" + e)
            self.cnt[e] = 0
        self.arena = self.nc.alloc_sbuf_tensor("arena", [128, ARENA_BYTES],
                                                mybir.dt.uint8)
        self._arena_base = self.nc.lookup_mloc(self.arena).addr
        self._sb_off = self._arena_base
        self._scratch_base = None
        self._scratch_off = None
        self._uid = 0
        self.tensors = {}
        self.offsets = {}

    def _alloc(self, name, shape, dtype, off):
        nbytes = int(np.prod(shape[1:])) * mybir.dt.size(dtype) if len(shape) > 1 else mybir.dt.size(dtype)
        t = self.nc.alloc_sbuf_tensor_at(name, list(shape), dtype, offset=off)
        self.tensors[name] = t
        self.offsets[id(t)] = off
        return t, off + ((nbytes + 31) // 32) * 32

    def persist(self, name, shape, dtype):
        assert self._scratch_base is None, "persist after scratch phase started"
        t, self._sb_off = self._alloc(name, shape, dtype, self._sb_off)
        assert self._sb_off <= self._arena_base + ARENA_BYTES, f"SBUF persist overflow {self._sb_off}"
        return t

    def begin_scratch(self):
        if self._scratch_base is None:
            self._scratch_base = self._sb_off
        self._scratch_off = self._scratch_base

    def scratch(self, name, shape, dtype):
        if self._scratch_off is None:
            self.begin_scratch()
        self._uid += 1
        t, self._scratch_off = self._alloc(f"{name}_{self._uid}", shape, dtype,
                                           self._scratch_off)
        assert self._scratch_off <= self._arena_base + ARENA_BYTES, \
            f"SBUF scratch overflow {self._scratch_off} ({name})"
        return t

    def reset_scratch(self):
        self._scratch_off = self._scratch_base

    def alias(self, other, name, shape, dtype, delta=0):
        self._uid += 1
        off = self.offsets[id(other)] + delta
        t = self.nc.alloc_sbuf_tensor_at(f"{name}_{self._uid}", list(shape), dtype,
                                         offset=off)
        self.offsets[id(t)] = off
        return t

    def op(self, eng, fn, waits=(), inc=False):
        byname = {}
        for ev in waits:
            if ev is None:
                continue
            if ev.val > byname.get(ev.sem, -1):
                byname[ev.sem] = ev.val
        if self.strict:
            inc = True
            if self.cnt[eng] > byname.get(eng, -1):
                byname[eng] = self.cnt[eng]
        w = []
        for name, val in byname.items():
            if name == eng and val >= self.cnt[eng] + (1 if inc else 0):
                continue
            if self.waited[eng].get(name, -1) >= val:
                continue
            self.waited[eng][name] = val
            w.append((self.sems[name], val))
        if inc:
            self.cnt[eng] += 1
        myval = self.cnt[eng]
        sem = self.sems[eng]

        def closure(engine, w=w, inc=inc, fn=fn):
            for s, v in w:
                engine.wait_ge(s, v)
            ins = fn(engine)
            if inc:
                ins.then_inc(sem, 1)
        self.phases[-1][eng].append(closure)
        return Ev(eng, myval) if inc else None

    def dma(self, dst, src, group="g", waits=(), queue="sync"):
        q = "dmag_" + group
        if q not in self.sems:
            self.sems[q] = self.nc.alloc_semaphore(q)
            self.cnt[q] = 0
        self.cnt[q] += 16
        sem = self.sems[q]
        myval = self.cnt[q]
        byname = {}
        for ev in waits:
            if ev is None:
                continue
            if ev.val > byname.get(ev.sem, -1):
                byname[ev.sem] = ev.val
        w = []
        for name, val in byname.items():
            if self.waited[queue].get(name, -1) >= val:
                continue
            self.waited[queue][name] = val
            w.append((self.sems[name], val))

        def closure(engine, w=w, dst=dst, src=src):
            for s, v in w:
                engine.wait_ge(s, v)
            engine.dma_start(dst, src).then_inc(sem, 16)
        self.phases[-1][queue].append(closure)
        return Ev(q, myval)

    def dma_ev(self, group):
        return Ev("dmag_" + group, self.cnt["dmag_" + group])

    def phase(self):
        if any(self.phases[-1][e] for e in ENGINES):
            self.phases.append({e: [] for e in ENGINES})

    def finalize(self):
        nc = self.nc
        for ph in self.phases:
            if not any(ph[e] for e in ENGINES):
                continue
            with nc.Block() as blk:
                for e in ENGINES:
                    lst = ph[e]
                    if not lst:
                        continue
                    getattr(blk, e)(lambda engine, lst=lst: [f(engine) for f in lst])
        return nc




def bf(x):
    return np.ascontiguousarray(x).astype(ml_dtypes.bfloat16)


def f32(x):
    return np.ascontiguousarray(x, dtype=np.float32)


class Ctx:

    def __init__(self, p: Prog, ins: dict):
        self.p = p
        self.nc = p.nc
        self.dram = {}
        self.in_maps = {}
        self.ins = ins
        self.resid = p.persist("resid", [128, NT * D], F32)
        self.ident = p.persist("ident", [128, 128], BF16)
        self.ones_row = p.persist("ones_row", [1, 512], BF16)
        self.ones_col = p.persist("ones_col", [128, 1], BF16)
        self.cpool = p.persist("cpool", [128, 32], F32)
        self._cpool_vals = {}
        self._cpool_arr = np.zeros((1, 32), np.float32)
        self.evs = {}

    def ccol(self, value):
        v = float(value)
        if v not in self._cpool_vals:
            idx = len(self._cpool_vals)
            assert idx < 32
            self._cpool_vals[v] = idx
            self._cpool_arr[0, idx] = v
        i = self._cpool_vals[v]
        return self.cpool[:, i:i + 1]

    def debug_dump(self, name, ap):
        if not hasattr(self, "_dumps"):
            self._dumps = []
        t = self.nc.dram_tensor("dump_" + name, list(ap.shape), ap.dtype,
                                kind="ExternalOutput").ap()
        self._dumps.append((t, ap))

    def debug_dump_now(self, name, ap, waits=()):
        t = self.nc.dram_tensor("dump_" + name, list(ap.shape), ap.dtype,
                                kind="ExternalOutput").ap()
        self.p.dma(t, ap, group="xout", queue="gpsimd", waits=waits)
        ev = self.p.dma_ev("xout")
        for e in ("vector", "tensor", "scalar", "gpsimd"):
            wait_only(self.p, e, [ev])
        if not hasattr(self, "_dump_now_names"):
            self._dump_now_names = []
        self._dump_now_names.append("dump_" + name)

    def flush_dumps(self):
        for t, ap in getattr(self, "_dumps", []):
            self.p.dma(t, ap, group="xout", queue="gpsimd")
        wait_only(self.p, "gpsimd", [self.p.dma_ev("xout")])

    def add_input(self, name, arr, dtype=None):
        if name in self.dram:
            return self.dram[name]
        a = np.ascontiguousarray(arr)
        if dtype is None:
            dtype = {np.dtype(np.float32): F32, np.dtype(np.int32): I32,
                     np.dtype(ml_dtypes.bfloat16): BF16}[a.dtype]
        t = self.nc.dram_tensor(name, list(a.shape), dtype, kind="ExternalInput").ap()
        self.dram[name] = t
        self.in_maps[name] = a
        return t

    def load_consts(self):
        p = self.p
        n = 128
        ident_np = bf(np.eye(n, dtype=np.float32))
        ones_np = bf(np.ones((1, 512), dtype=np.float32))
        onec_np = bf(np.ones((128, 1), dtype=np.float32))
        d1 = self.add_input("c_ident", ident_np)
        d2 = self.add_input("c_ones_row", ones_np)
        d3 = self.add_input("c_ones_col", onec_np)
        d4 = self.add_input("c_cpool", self._cpool_arr)
        e1 = p.dma(self.ident[:, :], d1, group="consts")
        e2 = p.dma(self.ones_row[:, :], d2, group="consts")
        e3 = p.dma(self.ones_col[:, :], d3, group="consts")
        e4 = p.dma(self.cpool[:, :], d4.broadcast_to([128, 32]), group="consts")
        return e4




def emit_ln_stats(ctx, src_tile_fn, mv, n_tiles=NT, width=D, ev_src=None, sqt=None):
    p = ctx.p
    n = n_tiles
    sums = p.scratch("lnsum", [128, 2 * n], F32)
    if sqt is None:
        sqt = p.scratch("lnsq", [128, width], F32)
    e2 = None
    for ti in range(n_tiles):
        src = src_tile_fn(ti)
        p.op("vector", lambda v, src=src, ti=ti: v.scalar_tensor_tensor(
            sqt[:, :], src, 1.0, src, ALU.bypass, ALU.mult,
            accum_out=sums[:, n + ti:n + ti + 1]),
            waits=[ev_src] if ev_src else ())
        e2 = p.op("vector", lambda v, src=src, ti=ti: v.reduce_sum(
            sums[:, ti:ti + 1], src, axis=AX), inc=True)
    inv = 1.0 / width
    mv_even = mv[:, 0:2 * n:2]
    mv_odd = mv[:, 1:2 * n:2]
    e3 = p.op("vector", lambda v: v.tensor_scalar(
        mv_even, sums[:, 0:n], inv, None, ALU.mult), waits=[e2], inc=True)
    e4 = p.op("vector", lambda v: v.tensor_tensor(
        sums[:, 0:n], mv_even, mv_even, ALU.mult), waits=[e3], inc=True)
    e5 = p.op("vector", lambda v: v.scalar_tensor_tensor(
        mv_odd, sums[:, n:2 * n], inv, sums[:, 0:n], ALU.mult, ALU.subtract),
        waits=[e4], inc=True)
    eps_ap = ctx.ccol(LN_EPS)
    e6 = p.op("scalar", lambda sc: sc.activation(
        mv_odd, mv_odd, AF.Ln, bias=eps_ap), waits=[e5], inc=True)
    e7 = p.op("scalar", lambda sc: sc.activation(
        mv_odd, mv_odd, AF.Exp, scale=-0.5), waits=[e6], inc=True)
    return e7


def emit_tm_to_fm(ctx, tm_tile_fn, fm, n_tiles=NT, n_chunks=ND, wait=(),
                  t0=0, psum=None, psum_ev=None):
    p = ctx.p
    ev = None
    pe_evs = {}
    nslots = 8
    ident = ctx.ident
    last_consume = [None] * nslots
    for ti in range(n_tiles):
        for ci in range(n_chunks):
            slot = (ti * n_chunks + ci) % nslots
            ps = psum[:, slot * 128:(slot + 1) * 128]
            w = list(wait) + [last_consume[slot]]
            e_pe = p.op("tensor", lambda t, ti=ti, ci=ci, ps=ps: t.transpose(
                ps, tm_tile_fn(ti)[:, ci * 128:(ci + 1) * 128], ident[:, :]),
                waits=w, inc=True)
            ev = p.op("vector", lambda v, ti=ti, ci=ci, ps=ps: v.tensor_copy(
                fm[:, ci * T + t0 + ti * 128: ci * T + t0 + (ti + 1) * 128], ps),
                waits=[e_pe], inc=True)
            last_consume[slot] = ev
    return ev


def wait_only(p, eng, evs):
    byname = {}
    for ev in evs:
        if ev is None:
            continue
        if ev.val > byname.get(ev.sem, -1):
            byname[ev.sem] = ev.val
    w = []
    for name, val in byname.items():
        if p.waited[eng].get(name, -1) >= val:
            continue
        p.waited[eng][name] = val
        w.append((p.sems[name], val))
    if not w:
        return

    def closure(engine, w=w):
        for s, v in w:
            engine.wait_ge(s, v)
    p.phases[-1][eng].append(closure)


def emit_round(ctx, x_ap, out_ap, qi, qf, dd, waits=()):
    p = ctx.p
    e1 = p.op("vector", lambda v: v.tensor_copy(qi, x_ap), waits=waits, inc=True)
    e2 = p.op("vector", lambda v: v.tensor_copy(qf, qi), waits=[e1], inc=True)
    e3 = p.op("vector", lambda v: v.tensor_tensor(dd, x_ap, qf, ALU.subtract),
              waits=[e2], inc=True)
    e4 = p.op("vector", lambda v: v.scalar_tensor_tensor(
        qf, dd, 0.5, qf, ALU.is_gt, ALU.add), waits=[e3], inc=True)
    e5 = p.op("vector", lambda v: v.tensor_single_scalar(
        dd, dd, -0.5, ALU.is_lt), waits=[e4], inc=True)
    e6 = p.op("vector", lambda v: v.tensor_tensor(out_ap, qf, dd, ALU.subtract),
              waits=[e5], inc=True)
    return e6


def emit_transpose8(ctx, src_fn, dst3d, wait=(), nblk=8):
    p = ctx.p
    bank = ctx.psb_next[0]
    ctx.psb_next[0] ^= 1
    base = bank * 1024
    w0 = list(wait) + [ctx.psb_rd[bank]]
    e_pe = None
    for i in range(nblk):
        e_pe = p.op("tensor", lambda t, i=i: t.transpose(
            ctx.psb[:, base + i * 128: base + (i + 1) * 128], src_fn(i),
            ctx.ident[:, :]), waits=(w0 if i == 0 else ()), inc=(i == nblk - 1))
    psv = ctx.psb[:, base:base + nblk * 128].rearrange("p (c t) -> p c t", t=128)
    e_cp = p.op("vector", lambda v: v.tensor_copy(dst3d, psv), waits=[e_pe], inc=True)
    ctx.psb_rd[bank] = e_cp
    return e_cp




def emit_l0ffn(ctx, w_T_dram, hada_dram, ev_w, ev_h):
    p = ctx.p
    psf = ctx.psf
    psb = ctx.psb

    wq = p.scratch("wq", [128, ND * 1024], BF16)
    hada = p.scratch("hada", [128, ND * 1024], BF16)
    wstage = p.scratch("wstage", [128, ND * 1024], F32)
    qT = p.alias(wstage, "qT", [128, ND * T], BF16)
    xnT = p.scratch("xnT", [128, ND * T], BF16)
    mv = p.scratch("mv", [128, 2 * NT], F32)
    scol = p.scratch("scol", [128, NT], F32)
    absw = p.alias(xnT, "absw", [128, ND * 1024], BF16, delta=0)
    wsrow = p.alias(xnT, "wsrow", [1, 1024], F32, delta=16 * 1024)
    wsrow16 = p.alias(xnT, "wsrow16", [1, 1024], BF16, delta=20 * 1024)
    hs = p.alias(xnT, "hs", [128, 1024], BF16, delta=22 * 1024)
    ws = p.alias(xnT, "ws", [128, 1024], BF16, delta=24 * 1024)
    tmp1 = p.alias(xnT, "tmp1", [128, 1024], BF16, delta=26 * 1024)
    tmp2 = p.alias(xnT, "tmp2", [128, 1024], BF16, delta=28 * 1024)
    xn_t = p.scratch("xn_t", [128, 2 * 1024], BF16)
    q_t = p.scratch("q_t", [128, 2 * 1024], BF16)
    qi_t = p.scratch("qi_t", [128, 1024], I32)
    qf_t = p.scratch("qf_t", [128, 1024], F32)
    dd_t = p.scratch("dd_t", [128, 1024], F32)
    xh_t = p.scratch("xh_t", [128, 1024], F32)

    for ci in range(ND):
        p.dma(wstage[:, ci * 1024:(ci + 1) * 1024],
              w_T_dram[ci * 128:(ci + 1) * 128, :], group="l0f_w")
        p.dma(hada[:, ci * 1024:(ci + 1) * 1024],
              hada_dram[ci * 128:(ci + 1) * 128, :], group="l0f_w")
    e_wst = e_hd = p.dma_ev("l0f_w")

    e_abs = p.op("scalar", lambda sc: sc.activation(
        absw[:, :], wstage[:, :], AF.Abs), waits=[e_wst], inc=True)
    e_cs = None
    for ci in range(ND):
        for oc in range(2):
            e_cs = p.op("tensor", lambda t, ci=ci, oc=oc: t.matmul(
                psf[0:1, oc * 512:(oc + 1) * 512], ctx.ones_col[:, :],
                absw[:, ci * 1024 + oc * 512: ci * 1024 + (oc + 1) * 512],
                start=(ci == 0), stop=(ci == ND - 1)),
                waits=[e_abs], inc=True)
    e_ws = p.op("vector", lambda v: v.tensor_scalar(
        wsrow[:, :], psf[0:1, 0:1024], 1.0 / 1024.0, 1e-5, ALU.mult, ALU.max),
        waits=[e_cs], inc=True)
    e_ws16 = p.op("vector", lambda v: v.tensor_copy(wsrow16[:, :], wsrow[:, :]),
                  waits=[e_ws], inc=True)
    e_b = None
    for oc in range(2):
        p.op("tensor", lambda t, oc=oc: t.matmul(
            psf[:, 1024 + oc * 512: 1024 + (oc + 1) * 512],
            ctx.half_row[:, :], wsrow16[0:1, oc * 512:(oc + 1) * 512],
            start=True, stop=True), waits=[e_ws16, e_cs], inc=True)
        e_b = p.op("tensor", lambda t, oc=oc: t.matmul(
            psf[:, 2048 + oc * 512: 2048 + (oc + 1) * 512],
            ctx.one_row[:, :], wsrow16[0:1, oc * 512:(oc + 1) * 512],
            start=True, stop=True), waits=[e_ws16], inc=True)
    e_hs = p.op("vector", lambda v: v.tensor_copy(hs[:, :], psf[:, 1024:2048]),
                waits=[e_b], inc=True)
    e_wsb = p.op("vector", lambda v: v.tensor_copy(ws[:, :], psf[:, 2048:3072]),
                 waits=[e_b], inc=True)
    ev_wq = None
    for ci in range(ND):
        sl = slice(ci * 1024, (ci + 1) * 1024)
        e_s = p.op("scalar", lambda sc, sl=sl: sc.activation(
            tmp1[:, :], wstage[:, sl], AF.Sign), waits=[e_wst, ev_wq], inc=True)
        e_m = p.op("vector", lambda v, sl=sl: v.tensor_tensor(
            tmp2[:, :], absw[:, sl], hs[:, :], ALU.is_gt), waits=[e_hs, e_s], inc=True)
        e_t = p.op("vector", lambda v: v.tensor_tensor(
            tmp1[:, :], tmp1[:, :], tmp2[:, :], ALU.mult), waits=[e_m], inc=True)
        ev_wq = p.op("vector", lambda v, sl=sl: v.tensor_tensor(
            wq[:, sl], tmp1[:, :], ws[:, :], ALU.mult), waits=[e_wsb, e_t], inc=True)

    ev_ln = emit_ln_stats(ctx, lambda ti: ctx.resid[:, ti * D:(ti + 1) * D], mv,
                          sqt=p.alias(qf_t, "lnsq", [128, 1024], F32))
    if getattr(ctx, "dev_l0ffn", False):
        ctx.debug_dump("wq", wq[:, :])
        ctx.debug_dump("mv", mv[:, :])
        ctx.debug_dump("xnT", xnT[:, :])
        ctx.debug_dump("qT", qT[:, :])
        ctx.debug_dump("scol", scol[:, :])
    p.phase()

    ev_prev_tile = {}
    last_reads = {}
    for ti in range(NT):
        par = ti % 2
        e_xn = p.op("vector", lambda v, ti=ti, par=par: v.tensor_scalar(
            xn_t[:, par * 1024:(par + 1) * 1024], ctx.resid[:, ti * D:(ti + 1) * D],
            mv[:, 2 * ti:2 * ti + 1], mv[:, 2 * ti + 1:2 * ti + 2],
            ALU.subtract, ALU.mult),
            waits=[ev_ln, last_reads.get(("xn", par))], inc=True)
        dst3d = xnT[:, :].rearrange("p (c t) -> p c t", t=T)[:, :, ti * 128:(ti + 1) * 128]
        e_tr = emit_transpose8(
            ctx, lambda ci, par=par: xn_t[:, par * 1024 + ci * 128: par * 1024 + (ci + 1) * 128],
            dst3d, wait=[e_xn])
        last_reads[("xn", par)] = e_tr
        e_h = None
        for oc in range(2):
            for ci in range(ND):
                e_h = p.op("tensor", lambda t, ti=ti, ci=ci, oc=oc, par=par: t.matmul(
                    psf[:, (par * 2 + oc) * 512:(par * 2 + oc + 1) * 512],
                    xnT[:, ci * T + ti * 128: ci * T + (ti + 1) * 128],
                    hada[:, ci * 1024 + oc * 512: ci * 1024 + (oc + 1) * 512],
                    start=(ci == 0), stop=(ci == ND - 1)),
                    waits=[e_tr, e_hd, last_reads.get(("ps", par * 2 + oc))], inc=True)
        psv = psf[:, par * 1024:(par + 1) * 1024]
        e_rm = p.op("vector", lambda v, ti=ti, psv=psv: v.reduce_max(
            scol[:, ti:ti + 1], psv, axis=AX, apply_absolute_value=True),
            waits=[e_h], inc=True)
        e_m2 = p.op("vector", lambda v, ti=ti: v.tensor_scalar(
            mv[:, 2 * ti:2 * ti + 1], scol[:, ti:ti + 1], 1e-5, None, ALU.max),
            waits=[e_rm], inc=True)
        e_r = p.op("vector", lambda v, ti=ti: v.reciprocal(
            scol[:, ti:ti + 1], mv[:, 2 * ti:2 * ti + 1]), waits=[e_m2], inc=True)
        e_q1 = p.op("vector", lambda v, ti=ti, psv=psv: v.tensor_scalar(
            xh_t[:, :], psv,
            scol[:, ti:ti + 1], 7.0, ALU.mult, ALU.mult),
            waits=[e_r, last_reads.get("xh")], inc=True)
        last_reads[("ps", par * 2)] = e_q1
        last_reads[("ps", par * 2 + 1)] = e_q1
        e_q3 = emit_round(ctx, xh_t[:, :],
                          q_t[:, par * 1024:(par + 1) * 1024],
                          qi_t[:, :], qf_t[:, :], dd_t[:, :],
                          waits=[e_q1, last_reads.get(("q", par))])
        last_reads["xh"] = e_q3
        e_sc = p.op("vector", lambda v, ti=ti: v.tensor_scalar(
            scol[:, ti:ti + 1], mv[:, 2 * ti:2 * ti + 1], 1.0 / 7.0, None, ALU.mult),
            waits=[e_q1], inc=True)
        dst3dq = qT[:, :].rearrange("p (c t) -> p c t", t=T)[:, :, ti * 128:(ti + 1) * 128]
        e_qt = emit_transpose8(
            ctx, lambda ci, par=par: q_t[:, par * 1024 + ci * 128: par * 1024 + (ci + 1) * 128],
            dst3dq, wait=[e_q3])
        last_reads[("q", par)] = e_qt
        for oc in range(2):
            e_mm = None
            for ci in range(ND):
                e_mm = p.op("tensor", lambda t, ti=ti, ci=ci, oc=oc: t.matmul(
                    psf[:, (4 + oc) * 512:(5 + oc) * 512],
                    qT[:, ci * T + ti * 128: ci * T + (ti + 1) * 128],
                    wq[:, ci * 1024 + oc * 512: ci * 1024 + (oc + 1) * 512],
                    start=(ci == 0), stop=(ci == ND - 1)),
                    waits=[e_qt, ev_wq, last_reads.get(("pso", oc))], inc=True)
            e_out = p.op("vector", lambda v, ti=ti, oc=oc: v.scalar_tensor_tensor(
                ctx.resid[:, ti * D + oc * 512: ti * D + (oc + 1) * 512],
                psf[:, (4 + oc) * 512:(5 + oc) * 512], scol[:, ti:ti + 1],
                ctx.resid[:, ti * D + oc * 512: ti * D + (oc + 1) * 512],
                ALU.mult, ALU.add),
                waits=[e_mm, e_sc], inc=True)
            last_reads[("pso", oc)] = e_out
    p.phase()


def emit_l1ffn(ctx, packedT_dram, wscale_dram, bias_dram, unpack=True):
    p = ctx.p
    psf = ctx.psf
    psb = ctx.psb

    wq = ctx.l1_wq
    pstage = p.scratch("pstage", [128, ND * 256], I32)
    enc = p.scratch("enc", [128, 256], I32)
    xnT = p.scratch("l1xnT", [128, ND * T], BF16)
    mv = p.scratch("l1mv", [128, 2 * NT], F32)
    scol = p.scratch("l1scol", [128, NT], F32)
    mcol = p.scratch("l1mcol", [128, NT], F32)
    wsb = p.scratch("wsb", [128, 1], F32)
    biasb = p.scratch("biasb", [128, 1024], F32)
    xnf_t = p.scratch("l1xnf", [128, 1024], F32)
    q_t = p.scratch("l1q", [128, 2 * 1024], BF16)
    qi_t = p.scratch("l1qi", [128, 1024], I32)
    qf_t = p.scratch("l1qf", [128, 1024], F32)
    dd_t = p.scratch("l1dd", [128, 1024], F32)
    t1_t = p.scratch("l1t1", [128, 2 * 1024], F32)

    ev_wq = None
    if unpack:
        for ci in range(ND):
            p.dma(pstage[:, ci * 256:(ci + 1) * 256],
                  packedT_dram[ci * 128:(ci + 1) * 128, :], group="l1f_w")
        e_pst = p.dma_ev("l1f_w")
        for ci in range(ND):
            for s in range(4):
                e_e = p.op("vector", lambda v, ci=ci, s=s: v.tensor_scalar(
                    enc[:, :], pstage[:, ci * 256:(ci + 1) * 256], 2 * s, 3,
                    ALU.logical_shift_right, ALU.bitwise_and),
                    waits=[e_pst, ev_wq], inc=True)
                ev_wq = p.op("vector", lambda v, ci=ci, s=s: v.tensor_scalar(
                    wq[:, ci * 1024 + s * 256: ci * 1024 + (s + 1) * 256],
                    enc[:, :], 1, None, ALU.subtract),
                    waits=[e_e], inc=True)
    p.dma(wsb[:, :], wscale_dram.broadcast_to([128, 1]), group="l1f_c")
    p.dma(biasb[:, :], bias_dram.broadcast_to([128, 1024]), group="l1f_c")
    e_ws = e_bias = p.dma_ev("l1f_c")

    ev_ln = emit_ln_stats(ctx, lambda ti: ctx.resid[:, ti * D:(ti + 1) * D], mv)
    p.phase()

    last_reads = {}
    for ti in range(NT):
        par = ti % 2
        e_xn = p.op("vector", lambda v, ti=ti: v.tensor_scalar(
            xnf_t[:, :], ctx.resid[:, ti * D:(ti + 1) * D],
            mv[:, 2 * ti:2 * ti + 1], mv[:, 2 * ti + 1:2 * ti + 2],
            ALU.subtract, ALU.mult),
            waits=[ev_ln, last_reads.get("xn")], inc=True)
        e_rm = p.op("vector", lambda v, ti=ti: v.reduce_max(
            mcol[:, ti:ti + 1], xnf_t[:, :],
            axis=AX, apply_absolute_value=True), waits=[e_xn], inc=True)
        e_m2 = p.op("vector", lambda v, ti=ti: v.tensor_scalar(
            mcol[:, ti:ti + 1], mcol[:, ti:ti + 1], 1e-5, None, ALU.max),
            waits=[e_rm], inc=True)
        e_r = p.op("vector", lambda v, ti=ti: v.reciprocal(
            scol[:, ti:ti + 1], mcol[:, ti:ti + 1]), waits=[e_m2], inc=True)
        e_q1 = p.op("vector", lambda v, ti=ti: v.tensor_scalar(
            xnf_t[:, :], xnf_t[:, :],
            scol[:, ti:ti + 1], 127.0, ALU.mult, ALU.mult),
            waits=[e_r], inc=True)
        e_q3 = emit_round(ctx, xnf_t[:, :],
                          q_t[:, par * 1024:(par + 1) * 1024],
                          qi_t[:, :], qf_t[:, :], dd_t[:, :],
                          waits=[e_q1, last_reads.get(("q", par))])
        last_reads["xn"] = e_q3
        e_sc = p.op("vector", lambda v, ti=ti: v.tensor_scalar(
            scol[:, ti:ti + 1], mcol[:, ti:ti + 1], wsb[:, 0:1], 1.0 / 127.0,
            ALU.mult, ALU.mult), waits=[e_ws, e_q1], inc=True)
        dst3dq = xnT[:, :].rearrange("p (c t) -> p c t", t=T)[:, :, ti * 128:(ti + 1) * 128]
        e_qt = emit_transpose8(
            ctx, lambda ci, par=par: q_t[:, par * 1024 + ci * 128: par * 1024 + (ci + 1) * 128],
            dst3dq, wait=[e_q3])
        last_reads[("q", par)] = e_qt
        for oc in range(2):
            e_mm = None
            for ci in range(ND):
                e_mm = p.op("tensor", lambda t, ti=ti, ci=ci, oc=oc, par=par: t.matmul(
                    psf[:, (par * 2 + oc) * 512:(par * 2 + oc + 1) * 512],
                    xnT[:, ci * T + ti * 128: ci * T + (ti + 1) * 128],
                    wq[:, ci * 1024 + oc * 512: ci * 1024 + (oc + 1) * 512],
                    start=(ci == 0), stop=(ci == ND - 1)),
                    waits=[e_qt, ev_wq, last_reads.get(("pso", par * 2 + oc))], inc=True)
            e_t1 = p.op("vector", lambda v, ti=ti, oc=oc, par=par: v.scalar_tensor_tensor(
                t1_t[:, par * 1024 + oc * 512: par * 1024 + (oc + 1) * 512],
                psf[:, (par * 2 + oc) * 512:(par * 2 + oc + 1) * 512],
                scol[:, ti:ti + 1], biasb[:, oc * 512:(oc + 1) * 512],
                ALU.mult, ALU.add),
                waits=[e_mm, e_sc, e_bias, last_reads.get(("t1", par, oc))], inc=True)
            last_reads[("pso", par * 2 + oc)] = e_t1
            e_out = p.op("vector", lambda v, ti=ti, oc=oc, par=par: v.tensor_tensor(
                ctx.resid[:, ti * D + oc * 512: ti * D + (oc + 1) * 512],
                ctx.resid[:, ti * D + oc * 512: ti * D + (oc + 1) * 512],
                t1_t[:, par * 1024 + oc * 512: par * 1024 + (oc + 1) * 512], ALU.add),
                waits=[e_t1], inc=True)
            last_reads[("t1", par, oc)] = e_out
    p.phase()




def _new_ctx(ins, name="prog", need_l1wq=True):
    p = Prog(name, strict=STRICT)
    ctx = Ctx(p, ins)
    ctx.psf = p.nc.alloc_psum_tensor("psf", [128, 3072], F32)
    ctx.psb = p.nc.alloc_psum_tensor("psb", [128, 2048], BF16)
    ctx.psb_rd = [None, None]
    ctx.psb_next = [0]
    ctx.half_row = p.persist("half_row", [1, 128], BF16)
    ctx.one_row = p.persist("one_row", [1, 128], BF16)
    ctx.identf = p.persist("identf", [128, 128], F32)
    ctx.onesf = p.persist("onesf", [1, 256], F32)
    ctx.l1_wq = p.persist("l1wq", [128, ND * 1024], BF16) if need_l1wq else None
    e = ctx.load_consts()
    ctx.add_input("c_half_row", bf(np.full((1, 128), 0.5, np.float32)))
    ctx.add_input("c_one_row", bf(np.ones((1, 128), np.float32)))
    ctx.add_input("c_identf", np.eye(128, dtype=np.float32))
    ctx.add_input("c_onesf", np.ones((1, 256), np.float32))
    e1 = p.dma(ctx.half_row[:, :], ctx.dram["c_half_row"], group="consts")
    e2 = p.dma(ctx.one_row[:, :], ctx.dram["c_one_row"], group="consts")
    p.dma(ctx.identf[:, :], ctx.dram["c_identf"], group="consts")
    e2b = p.dma(ctx.onesf[:, :], ctx.dram["c_onesf"], group="consts")
    e2 = e2b
    wait_only(p, "vector", [e, e1, e2])
    wait_only(p, "tensor", [e, e1, e2])
    wait_only(p, "scalar", [e, e1, e2])
    p.phase()
    return ctx


def _load_resid(ctx, x_dram):
    p = ctx.p
    ev = None
    for ti in range(NT):
        ev = p.dma(ctx.resid[:, ti * D:(ti + 1) * D],
                   x_dram[ti * 128:(ti + 1) * 128, :], group="xin")
    wait_only(p, "vector", [ev])
    wait_only(p, "tensor", [ev])
    wait_only(p, "scalar", [ev])
    p.phase()


def _store_resid(ctx, y_dram):
    p = ctx.p
    ev = None
    for ti in range(NT):
        ev = p.dma(y_dram[ti * 128:(ti + 1) * 128, :],
                   ctx.resid[:, ti * D:(ti + 1) * D], group="xout", queue="gpsimd")
    wait_only(p, "gpsimd", [ev])




def emit_w16_stream(ctx, w16, dram, col0, ncols, group, waits=(), nchunks=ND,
                    row0=0):
    p = ctx.p
    if dram.dtype == BF16:
        g = group + "d"
        for ci in range(nchunks):
            p.dma(w16[:, ci * ncols:(ci + 1) * ncols],
                  dram[row0 + ci * 128: row0 + (ci + 1) * 128, col0:col0 + ncols],
                  group=g, waits=list(waits))
        return p.dma_ev(g)
    stage = ctx.wcvt_stage
    ev_cvt = {}
    ev = None
    for ci in range(nchunks):
        par = ci % 2
        g = f"{group}{par}"
        p.dma(stage[:, par * ncols:(par + 1) * ncols],
              dram[row0 + ci * 128: row0 + (ci + 1) * 128, col0:col0 + ncols],
              group=g, waits=[ev_cvt.get(par)] + list(waits))
        e_d = p.dma_ev(g)
        ev = p.op("gpsimd", lambda g_, ci=ci, par=par: g_.tensor_copy(
            w16[:, ci * ncols:(ci + 1) * ncols],
            stage[:, par * ncols:(par + 1) * ncols]),
            waits=[e_d], inc=True)
        ev_cvt[par] = ev
    return ev


def emit_l0attna(ctx, wqkvT_dram, wprojT_dram, bproj_dram, ksmask_dram):
    p = ctx.p
    psf = ctx.psf
    EPS = float(np.finfo(np.float32).eps)

    ctx.wcvt_stage = p.scratch("aawst", [128, 2 * 1024], F32)
    mv = p.scratch("aamv", [128, 2 * NT], F32)
    ynT = p.scratch("aaynT", [128, ND * T], BF16)
    k_tm = p.scratch("aak", [128, NT * D], BF16)
    v_tm = p.scratch("aav", [128, NT * D], BF16)
    if getattr(ctx, "l1wq_free", False):
        w16 = p.alias(ctx.l1_wq, "aaw16", [128, ND * 1024], BF16)
    else:
        w16 = p.scratch("aaw16", [128, ND * 1024], BF16)
    xn_t = p.scratch("aaxn", [128, 2 * 1024], BF16)
    ksr = p.scratch("aaksr", [1, 1024], F32)
    ksr16 = p.scratch("aaksr16", [1, 1024], BF16)
    kscol = p.scratch("aakscol", [128, 8], F32)
    ksM = p.scratch("aaksM", [128, ND * 16], BF16)
    ctxe = p.scratch("aactxe", [128, 8 * 65], BF16)
    zden = p.scratch("aazden", [128, NT * 16], F32)
    brow = p.scratch("aabrow", [1, 1024], BF16)
    brstage = p.alias(ksr, "aabrst", [1, 1024], F32)
    q_fm = p.alias(k_tm, "aaq", [128, ND * T], BF16)
    attn_tm = p.alias(v_tm, "aaattn", [128, NT * D], BF16)
    attn_fm = p.alias(ynT, "aaattnf", [128, ND * T], BF16)

    ev_ln = emit_ln_stats(ctx, lambda ti: ctx.resid[:, ti * D:(ti + 1) * D], mv,
                          sqt=p.alias(ynT, "aalnsq", [128, 1024], F32))
    ev_wk = emit_w16_stream(ctx, w16, wqkvT_dram, 1024, 1024, "aaw")
    p.dma(brstage[:, :], bproj_dram, group="aab")
    e_brd = p.dma_ev("aab")
    e_br = p.op("gpsimd", lambda g: g.tensor_copy(brow[:, :], brstage[:, :]),
                waits=[e_brd], inc=True)
    p.phase()

    last = {}
    e_yt = None
    for ti in range(NT):
        par = ti % 2
        e_xn = p.op("vector", lambda v, ti=ti, par=par: v.tensor_scalar(
            xn_t[:, par * 1024:(par + 1) * 1024], ctx.resid[:, ti * D:(ti + 1) * D],
            mv[:, 2 * ti:2 * ti + 1], mv[:, 2 * ti + 1:2 * ti + 2],
            ALU.subtract, ALU.mult),
            waits=[ev_ln, last.get(("xn", par))], inc=True)
        dst3d = ynT[:, :].rearrange("p (c t) -> p c t", t=T)[:, :, ti * 128:(ti + 1) * 128]
        e_yt = emit_transpose8(
            ctx, lambda ci, par=par: xn_t[:, par * 1024 + ci * 128: par * 1024 + (ci + 1) * 128],
            dst3d, wait=[e_xn])
        last[("xn", par)] = e_yt
    p.phase()

    def emit_tm_proj(dst_tm, wsec_ev, act_func, dst_dtype_note=None):
        ev_done = None
        lr = {}
        for ti in range(NT):
            par = ti % 2
            for oc in range(2):
                bank = par * 2 + oc
                e_mm = None
                for ci in range(ND):
                    e_mm = p.op("tensor", lambda t, ti=ti, ci=ci, oc=oc, bank=bank: t.matmul(
                        psf[:, bank * 512:(bank + 1) * 512],
                        ynT[:, ci * T + ti * 128: ci * T + (ti + 1) * 128],
                        w16[:, ci * 1024 + oc * 512: ci * 1024 + (oc + 1) * 512],
                        start=(ci == 0), stop=(ci == ND - 1)),
                        waits=[wsec_ev, lr.get(bank)], inc=True)
                ev_done = p.op("scalar", lambda sc, ti=ti, oc=oc, bank=bank: sc.activation(
                    dst_tm[:, ti * D + oc * 512: ti * D + (oc + 1) * 512],
                    psf[:, bank * 512:(bank + 1) * 512], act_func),
                    waits=[e_mm], inc=True)
                lr[bank] = ev_done
        return ev_done

    e_k = emit_tm_proj(k_tm, ev_wk, AF.Relu)
    ev_wv = emit_w16_stream(ctx, w16, wqkvT_dram, 2048, 1024, "aaw", waits=[e_k])
    p.phase()

    e_ks = None
    for oc in range(2):
        for ti in range(NT):
            e_ks = p.op("tensor", lambda t, ti=ti, oc=oc: t.matmul(
                psf[0:1, 4 * 512 + oc * 512: 4 * 512 + (oc + 1) * 512],
                ctx.ones_col[:, :],
                k_tm[:, ti * D + oc * 512: ti * D + (oc + 1) * 512],
                start=(ti == 0), stop=(ti == NT - 1)),
                waits=[e_k], inc=True)
    e_ksr = p.op("vector", lambda v: v.tensor_copy(ksr[:, :], psf[0:1, 2048:3072]),
                 waits=[e_ks], inc=True)
    e_ksr16 = p.op("vector", lambda v: v.tensor_copy(ksr16[:, :], ksr[:, :]),
                   waits=[e_ksr], inc=True)
    e_kc = None
    for ci in range(ND):
        e_kc = p.op("tensor", lambda t, ci=ci: t.matmul(
            psf[:, ci: ci + 1],
            ksr16[0:1, ci * 128:(ci + 1) * 128], ctx.ones_row[0:1, 0:1],
            start=True, stop=True),
            waits=[e_ksr16], inc=True)
    for h in range(16):
        b0 = (h % 2) * 64
        e_kc = p.op("tensor", lambda t, h=h, b0=b0: t.matmul(
            psf[b0:b0 + 64, 8 + h // 2: 9 + h // 2],
            ksr16[0:1, h * 64:(h + 1) * 64], ctx.ones_row[0:1, 0:1],
            start=True, stop=True, tile_position=(0, b0)),
            waits=[e_ksr16], inc=True)
    e_kcc = p.op("vector", lambda v: v.tensor_copy(kscol[:, :], psf[:, 0:8]),
                 waits=[e_kc], inc=True)
    p.dma(ksM[:, :], ksmask_dram, group="aab")
    e_kml = p.dma_ev("aab")
    e_km = None
    for ci in range(ND):
        e_km = p.op("vector", lambda v, ci=ci: v.tensor_scalar(
            ksM[:, ci * 16:(ci + 1) * 16], ksM[:, ci * 16:(ci + 1) * 16],
            kscol[:, ci:ci + 1], None, ALU.mult),
            waits=[e_kcc, e_kml], inc=True)
    p.phase()

    e_v = emit_tm_proj(v_tm, ev_wv, AF.Identity)
    e_ctx = None
    for h in range(16):
        b0 = (h % 2) * 64
        col = (h // 2) * 64
        for ti in range(NT):
            e_ctx = p.op("tensor", lambda t, h=h, ti=ti, b0=b0, col=col: t.matmul(
                psf[b0:b0 + 64, 4 * 512 + col: 4 * 512 + col + 64],
                k_tm[:, ti * D + h * 64: ti * D + (h + 1) * 64],
                v_tm[:, ti * D + h * 64: ti * D + (h + 1) * 64],
                start=(ti == 0), stop=(ti == NT - 1), tile_position=(0, b0)),
                waits=[e_v, e_km, e_kcc], inc=True)
    e_ctxe = None
    for h in range(16):
        b0 = (h % 2) * 64
        col = (h // 2) * 64
        e_ctxe = p.op("vector", lambda v, h=h, b0=b0, col=col: v.tensor_copy(
            ctxe[b0:b0 + 64, (h // 2) * 65: (h // 2) * 65 + 64],
            psf[b0:b0 + 64, 4 * 512 + col: 4 * 512 + col + 64]),
            waits=[e_ctx], inc=True)
        e_ctxe = p.op("vector", lambda v, h=h, b0=b0: v.tensor_copy(
            ctxe[b0:b0 + 64, (h // 2) * 65 + 64: (h // 2) * 65 + 65],
            psf[b0:b0 + 64, 8 + h // 2: 9 + h // 2]),
            waits=[e_ctxe], inc=True)
    p.phase()

    ev_wq = emit_w16_stream(ctx, w16, wqkvT_dram, 0, 1024, "aaw")
    lrq = {}
    e_q = None
    for co in range(ND):
        for tt in range(4):
            bank = (co * 4 + tt) % 4
            e_mm = None
            for ci in range(ND):
                e_mm = p.op("tensor", lambda t, co=co, tt=tt, ci=ci, bank=bank: t.matmul(
                    psf[:, bank * 512:(bank + 1) * 512],
                    w16[:, ci * 1024 + co * 128: ci * 1024 + (co + 1) * 128],
                    ynT[:, ci * T + tt * 512: ci * T + (tt + 1) * 512],
                    start=(ci == 0), stop=(ci == ND - 1)),
                    waits=[ev_wq, lrq.get(bank)], inc=True)
            e_q = p.op("scalar", lambda sc, co=co, tt=tt, bank=bank: sc.activation(
                q_fm[:, co * T + tt * 512: co * T + (tt + 1) * 512],
                psf[:, bank * 512:(bank + 1) * 512], AF.Relu),
                waits=[e_mm], inc=True)
            lrq[bank] = e_q
    p.phase()

    e_d = None
    lrd = {}
    for ti in range(NT):
        bank = 4 + ti % 2
        e_mm = None
        for ci in range(ND):
            e_mm = p.op("tensor", lambda t, ti=ti, ci=ci, bank=bank: t.matmul(
                psf[0:128, bank * 512: bank * 512 + 16],
                q_fm[:, ci * T + ti * 128: ci * T + (ti + 1) * 128],
                ksM[:, ci * 16:(ci + 1) * 16],
                start=(ci == 0), stop=(ci == ND - 1)),
                waits=[e_q, lrd.get(bank)], inc=True)
        e_d = p.op("vector", lambda v, ti=ti, bank=bank: v.tensor_copy(
            zden[:, ti * 16:(ti + 1) * 16], psf[:, bank * 512: bank * 512 + 16]),
            waits=[e_mm], inc=True)
        lrd[bank] = e_d
    e_z1 = p.op("vector", lambda v: v.tensor_single_scalar(
        zden[:, :], zden[:, :], EPS, ALU.add), waits=[e_d], inc=True)
    e_z = p.op("vector", lambda v: v.reciprocal(zden[:, :], zden[:, :]),
               waits=[e_z1], inc=True)
    p.phase()

    ev_wp = emit_w16_stream(ctx, w16, wprojT_dram, 0, 1024, "aaw")
    bank_rd = {}
    for ti in range(NT):
        for bh in range(2):
            bank = (ti % 2) * 2 + bh
            e_mm = None
            for hh in range(8):
                h = bh * 8 + hh
                col = hh * 64
                b0 = (h % 2) * 64
                e_mm = p.op("tensor", lambda t, ti=ti, h=h, bank=bank, col=col, b0=b0: t.matmul(
                    psf[:, bank * 512 + col: bank * 512 + col + 64],
                    q_fm[b0:b0 + 64,
                         (h // 2) * T + ti * 128: (h // 2) * T + (ti + 1) * 128],
                    ctxe[b0:b0 + 64, (h // 2) * 65: (h // 2) * 65 + 64],
                    start=True, stop=True, tile_position=(b0, 0)),
                    waits=[e_z, bank_rd.get(bank)], inc=True)
            e_rd = None
            for hh in range(8):
                h = bh * 8 + hh
                col = hh * 64
                e_rd = p.op("vector", lambda v, ti=ti, h=h, bank=bank, col=col: v.tensor_scalar(
                    attn_tm[:, ti * D + h * 64: ti * D + (h + 1) * 64],
                    psf[:, bank * 512 + col: bank * 512 + col + 64],
                    zden[:, ti * 16 + h: ti * 16 + h + 1], None, ALU.mult),
                    waits=[e_mm], inc=True)
            bank_rd[bank] = e_rd
    p.phase()

    e_af = None
    for ti in range(NT):
        dst3d = attn_fm[:, :].rearrange("p (c t) -> p c t", t=T)[:, :, ti * 128:(ti + 1) * 128]
        e_af = emit_transpose8(
            ctx, lambda ci, ti=ti: attn_tm[:, ti * D + ci * 128: ti * D + (ci + 1) * 128],
            dst3d, wait=[])
    p.phase()
    lro = {}
    for ti in range(NT):
        par = ti % 2
        for oc in range(2):
            bank = par * 2 + oc
            e_b = p.op("tensor", lambda t, ti=ti, oc=oc, bank=bank: t.matmul(
                psf[:, bank * 512:(bank + 1) * 512],
                ctx.one_row[:, :], brow[0:1, oc * 512:(oc + 1) * 512],
                start=True, stop=False),
                waits=[e_af, e_br, lro.get(bank)], inc=True)
            e_mm = None
            for ci in range(ND):
                e_mm = p.op("tensor", lambda t, ti=ti, oc=oc, ci=ci, bank=bank: t.matmul(
                    psf[:, bank * 512:(bank + 1) * 512],
                    attn_fm[:, ci * T + ti * 128: ci * T + (ti + 1) * 128],
                    w16[:, ci * 1024 + oc * 512: ci * 1024 + (oc + 1) * 512],
                    start=False, stop=(ci == ND - 1)),
                    waits=[ev_wp], inc=True)
            lro[bank] = p.op("vector", lambda v, ti=ti, oc=oc, bank=bank: v.tensor_tensor(
                ctx.resid[:, ti * D + oc * 512: ti * D + (oc + 1) * 512],
                ctx.resid[:, ti * D + oc * 512: ti * D + (oc + 1) * 512],
                psf[:, bank * 512:(bank + 1) * 512], ALU.add),
                waits=[e_mm], inc=True)
    p.phase()




def emit_l1attna(ctx, wqT_dram, wkT_dram, wvT_dram, woT_dram, bias_dram, gb_dram):
    p = ctx.p
    psf = ctx.psf

    ctx.ulast = {}
    ctx.wcvt_stage = p.scratch("abwst", [128, 2 * 512], F32)
    stg_q = p.scratch("abstq", [128, 2 * 128], F32)
    stg_k = p.scratch("abstk", [128, 2 * 128], F32)
    stg_v = p.scratch("abstv", [128, 2 * 128], F32)
    if getattr(ctx, "l1wq_free", False):
        stg_o = p.alias(ctx.l1_wq, "absto", [128, 2 * 1024], F32)
    else:
        stg_o = p.scratch("absto", [128, 2 * 1024], F32)
    mv = p.scratch("abmv", [128, 2 * NT], F32)
    mv2 = p.scratch("abmv2", [128, 2 * NT], F32)
    t2b = p.scratch("abt2b", [128, 1024], F32)
    xn_t = p.scratch("abxn", [128, 1024], BF16)
    ynT = p.scratch("abynT", [128, ND * T], BF16)
    attn_raw = p.scratch("abattn", [128, NT * 16 * 65], BF16)
    qp = p.scratch("abqp", [128, T], BF16)
    kp = p.scratch("abkp", [128, T], BF16)
    vp = p.scratch("abvp", [128, NT * 130], BF16)
    w16q = p.scratch("abw16q", [128, ND * 128], BF16)
    w16k = p.scratch("abw16k", [128, ND * 128], BF16)
    w16v = p.scratch("abw16v", [128, ND * 128], BF16)
    u_t = p.scratch("abu", [128, 4 * 512], BF16)
    ctx_sb = p.scratch("abctx", [65, 2 * 512], BF16)
    denz = p.scratch("abdenz", [128, NT * 16], F32)
    rows = p.scratch("abrows", [1, 4 * 1024], BF16)
    rstage = p.alias(attn_raw, "abrst", [1, 4 * 1024], F32)
    if getattr(ctx, "l1wq_free", False):
        gb = p.alias(ctx.l1_wq, "abgb", [128, 2 * 1024], F32, delta=8192)
    else:
        gb = p.scratch("abgb", [128, 2 * 1024], F32)
    o_tmp = p.scratch("abotmp", [128, 1024], F32)
    t1b = p.scratch("abt1", [128, 1024], F32)
    attn_n = p.scratch("abattnn", [128, 1024], BF16)
    w16o = p.alias(qp, "abw16o", [128, ND * 1024], BF16)
    attn_fm = p.alias(ynT, "abattnf", [128, ND * T], BF16)
    NEG_BIAS = -12.0

    ev_ln = emit_ln_stats(ctx, lambda ti: ctx.resid[:, ti * D:(ti + 1) * D], mv,
                          sqt=p.alias(ynT, "ablnsq", [128, 1024], F32))
    p.dma(rstage[0:1, 0:1024], bias_dram[0:1], group="abc")
    p.dma(rstage[0:1, 1024:2048], bias_dram[1:2], group="abc")
    p.dma(rstage[0:1, 2048:3072], bias_dram[2:3], group="abc")
    p.dma(rstage[0:1, 3072:4096], bias_dram[3:4], group="abc")
    p.dma(gb[:, 0:1024], gb_dram[0:1].broadcast_to([128, 1024]), group="abc")
    p.dma(gb[:, 1024:2048], gb_dram[1:2].broadcast_to([128, 1024]), group="abc")
    e_rc = p.dma_ev("abc")
    e_rows = p.op("gpsimd", lambda g: g.tensor_copy(rows[:, :], rstage[:, :]),
                  waits=[e_rc], inc=True)
    p.phase()

    last = {}
    for ti in range(NT):
        e_xn = p.op("vector", lambda v, ti=ti: v.tensor_scalar(
            xn_t[:, :], ctx.resid[:, ti * D:(ti + 1) * D],
            mv[:, 2 * ti:2 * ti + 1], mv[:, 2 * ti + 1:2 * ti + 2],
            ALU.subtract, ALU.mult),
            waits=[ev_ln, last.get("xn")], inc=True)
        dst3d = ynT[:, :].rearrange("p (c t) -> p c t", t=T)[:, :, ti * 128:(ti + 1) * 128]
        last["xn"] = emit_transpose8(
            ctx, lambda ci: xn_t[:, ci * 128:(ci + 1) * 128], dst3d, wait=[e_xn])
    e_vo = p.op("vector", lambda v: v.memset(
        vp[:, :].rearrange("p (ti c) -> p ti c", c=130)[:, :, 64:130:65], 1.0),
        inc=True)
    p.phase()

    for pair in range(8):
        ev_wq = emit_w16_stream2(ctx, w16q, wqT_dram, pair * 128, 128, "abw",
                                 stage=stg_q)
        ev_wk = emit_w16_stream2(ctx, w16k, wkT_dram, pair * 128, 128, "abw2",
                                 stage=stg_k)
        ev_wv = emit_w16_stream2(ctx, w16v, wvT_dram, pair * 128, 128, "abw3",
                                 stage=stg_v)
        lr = {}
        for (dst, wsl, ev_w, bcol) in ((qp, w16q, ev_wq, 0), (kp, w16k, ev_wk, 1024)):
            for tt in range(4):
                bank = tt % 2 + (0 if dst is qp else 2)
                e_b = p.op("tensor", lambda t, tt=tt, bank=bank, bcol=bcol, pair=pair: t.matmul(
                    psf[:, bank * 512:(bank + 1) * 512],
                    rows[0:1, bcol + pair * 128: bcol + (pair + 1) * 128],
                    ctx.ones_row[0:1, :],
                    start=True, stop=False),
                    waits=[ev_w, e_rows, lr.get(bank)], inc=True)
                e_mm = None
                for ci in range(ND):
                    e_mm = p.op("tensor", lambda t, tt=tt, ci=ci, bank=bank, wsl=wsl: t.matmul(
                        psf[:, bank * 512:(bank + 1) * 512],
                        wsl[:, ci * 128:(ci + 1) * 128],
                        ynT[:, ci * T + tt * 512: ci * T + (tt + 1) * 512],
                        start=False, stop=(ci == ND - 1)), inc=True)
                lr[bank] = p.op("vector", lambda v, tt=tt, bank=bank, dst=dst: v.tensor_copy(
                    dst[:, tt * 512:(tt + 1) * 512], psf[:, bank * 512:(bank + 1) * 512]),
                    waits=[e_mm], inc=True)
        for ti in range(NT):
            bank = 4 + ti % 2
            e_b = p.op("tensor", lambda t, ti=ti, bank=bank, pair=pair: t.matmul(
                psf[:, bank * 512: bank * 512 + 128],
                ctx.one_row[:, :], rows[0:1, 2048 + pair * 128: 2048 + (pair + 1) * 128],
                start=True, stop=False),
                waits=[ev_wv, e_rows, e_vo, lr.get(bank)], inc=True)
            e_mm = None
            for ci in range(ND):
                e_mm = p.op("tensor", lambda t, ti=ti, ci=ci, bank=bank: t.matmul(
                    psf[:, bank * 512: bank * 512 + 128],
                    ynT[:, ci * T + ti * 128: ci * T + (ti + 1) * 128],
                    w16v[:, ci * 128:(ci + 1) * 128],
                    start=False, stop=(ci == ND - 1)), inc=True)
            e1 = p.op("vector", lambda v, ti=ti, bank=bank: v.tensor_copy(
                vp[:, ti * 130: ti * 130 + 64], psf[:, bank * 512: bank * 512 + 64]),
                waits=[e_mm], inc=True)
            lr[bank] = p.op("vector", lambda v, ti=ti, bank=bank: v.tensor_copy(
                vp[:, ti * 130 + 65: ti * 130 + 129],
                psf[:, bank * 512 + 64: bank * 512 + 128]),
                waits=[e1], inc=True)
        p.phase()

        if getattr(ctx, "dev_l1attna", False) and pair == 0:
            ctx.debug_dump_now("ab_rows", rows[:, :])
            ctx.debug_dump_now("ab_qp0", qp[:, :])
            ctx.debug_dump_now("ab_kp0", kp[:, :])
            ctx.debug_dump_now("ab_vp0", vp[:, :])
        neg12 = ctx.ccol(NEG_BIAS)
        pend = []
        for tt in range(4):
            ctx_bank = {0: 4, 1: 5}
            first_si = {0: True, 1: True}
            for si in range(4 * tt + 4):
                while sum(1 for f in pend if getattr(f, "is_ctx", False)) > 1:
                    pend.pop(0)()
                e_es = {}
                for hh in range(2):
                    b0 = hh * 64
                    sbank = (si % 2) * 2 + hh
                    uslot = ((si % 2) * 2 + hh) * 512
                    e_s = p.op("tensor", lambda t, si=si, tt=tt, b0=b0, sbank=sbank: t.matmul(
                        psf[:, sbank * 512:(sbank + 1) * 512],
                        kp[b0:b0 + 64, si * 128:(si + 1) * 128],
                        qp[b0:b0 + 64, tt * 512:(tt + 1) * 512],
                        start=True, stop=True, tile_position=(b0, 0)),
                        waits=[ctx.ulast.get(("u", uslot))], inc=True)
                    e_e = p.op("scalar", lambda sc, sbank=sbank, uslot=uslot: sc.activation(
                        u_t[:, uslot:uslot + 512], psf[:, sbank * 512:(sbank + 1) * 512],
                        AF.Exp, bias=neg12, scale=0.125),
                        waits=[e_s], inc=True)
                    if si >= 4 * tt:
                        e_e = p.op("gpsimd", lambda g, si=si, tt=tt, uslot=uslot: g.affine_select(
                            u_t[:, uslot:uslot + 512], u_t[:, uslot:uslot + 512],
                            [[1, 512]], ALU.is_ge, 0.0,
                            base=tt * 512 - si * 128, channel_multiplier=-1),
                            waits=[e_e], inc=True)
                    e_es[hh] = e_e

                def ctx_step(si=si, tt=tt, e_es=e_es,
                             fs=(first_si[0]), ctx_bank=ctx_bank):
                    for hh in range(2):
                        uslot = ((si % 2) * 2 + hh) * 512
                        e_c = p.op("tensor", lambda t, si=si, hh=hh, tt=tt,
                                   uslot=uslot, cb=ctx_bank[hh], fs=fs: t.matmul(
                            psf[0:65, cb * 512:(cb + 1) * 512],
                            vp[:, si * 130 + hh * 65: si * 130 + hh * 65 + 65],
                            u_t[:, uslot:uslot + 512],
                            start=fs, stop=(si == 4 * tt + 3)),
                            waits=[e_es[hh]] + ([ctx.ulast.get(("ctxps", hh))]
                                                if fs else []), inc=True)
                        ctx.ulast[("u", uslot)] = e_c
                        ctx.ulast[("lec", hh)] = e_c
                ctx_step.is_ctx = True
                first_si[0] = first_si[1] = False
                pend.append(ctx_step)

            def evac_tt(tt=tt, pair=pair, ctx_bank=ctx_bank):
                for hh in range(2):
                    h = pair * 2 + hh
                    cb = ctx_bank[hh]
                    par = hh
                    e_cp = p.op("vector", lambda v, cb=cb, par=par, hh=hh: v.tensor_copy(
                        ctx_sb[:, par * 512:(par + 1) * 512],
                        psf[0:65, cb * 512:(cb + 1) * 512]),
                        waits=[ctx.ulast.get(("ctxsb", par)),
                               ctx.ulast.get(("lec", hh))], inc=True)
                    ctx.ulast[("ctxps", hh)] = e_cp
                    for ts4 in range(4):
                        ti = tt * 4 + ts4
                        dst = attn_raw[:, ti * 1040 + h * 65: ti * 1040 + (h + 1) * 65]
                        e_t = emit_transpose1(ctx, ctx_sb[0:65, par * 512 + ts4 * 128:
                                                           par * 512 + (ts4 + 1) * 128],
                                              dst, wait=[e_cp], rows=65)
                    ctx.ulast[("ctxsb", par)] = e_t
            pend.append(evac_tt)
        for f in pend:
            f()
        p.phase()

    if getattr(ctx, "dev_l1attna", False):
        ctx.debug_dump("attn_raw", attn_raw[:, :])
        ctx.debug_dump_now("ab_ynT", ynT[:, :])
    e_g = None
    for ti in range(NT):
        src = attn_raw[:, ti * 1040 + 64: (ti + 1) * 1040: 65]
        e_g = p.op("vector", lambda v, ti=ti, src=src: v.tensor_copy(
            denz[:, ti * 16:(ti + 1) * 16], src), inc=True)
    e_z = p.op("vector", lambda v: v.reciprocal(denz[:, :], denz[:, :]),
               waits=[e_g], inc=True)
    for ti in range(NT):
        e_n = None
        for h in range(16):
            e_n = p.op("vector", lambda v, ti=ti, h=h: v.tensor_scalar(
                attn_n[:, h * 64:(h + 1) * 64],
                attn_raw[:, ti * 1040 + h * 65: ti * 1040 + h * 65 + 64],
                denz[:, ti * 16 + h: ti * 16 + h + 1], None, ALU.mult),
                waits=[e_z, last.get("attn_n")], inc=True)
        dst3d = attn_fm[:, :].rearrange("p (c t) -> p c t", t=T)[:, :, ti * 128:(ti + 1) * 128]
        last["attn_n"] = emit_transpose8(
            ctx, lambda ci: attn_n[:, ci * 128:(ci + 1) * 128], dst3d, wait=[e_n])
    p.phase()
    if getattr(ctx, "dev_l1attna", False):
        ctx.debug_dump_now("ab_attnfm", attn_fm[:, :])
        ctx.debug_dump_now("ab_denz", denz[:, :])

    ctx.wcvt_stage = stg_o
    ev_wo = emit_w16_stream(ctx, w16o, woT_dram, 0, 1024, "abwo")
    lro = {}
    for ti in range(NT):
        par = ti % 2
        for oc in range(2):
            bank = par * 2 + oc
            e_b = p.op("tensor", lambda t, ti=ti, oc=oc, bank=bank: t.matmul(
                psf[:, bank * 512:(bank + 1) * 512],
                ctx.one_row[:, :], rows[0:1, 3072 + oc * 512: 3072 + (oc + 1) * 512],
                start=True, stop=False),
                waits=[ev_wo, lro.get(bank)], inc=True)
            e_mm = None
            for ci in range(ND):
                e_mm = p.op("tensor", lambda t, ti=ti, oc=oc, ci=ci, bank=bank: t.matmul(
                    psf[:, bank * 512:(bank + 1) * 512],
                    attn_fm[:, ci * T + ti * 128: ci * T + (ti + 1) * 128],
                    w16o[:, ci * 1024 + oc * 512: ci * 1024 + (oc + 1) * 512],
                    start=False, stop=(ci == ND - 1)), inc=True)
            lro[bank] = p.op("vector", lambda v, ti=ti, oc=oc, bank=bank: v.tensor_copy(
                o_tmp[:, oc * 512:(oc + 1) * 512], psf[:, bank * 512:(bank + 1) * 512]),
                waits=[e_mm, last.get("otmp")], inc=True)
        e_s1 = p.op("vector", lambda v, ti=ti: v.scalar_tensor_tensor(
            t1b[:, :], o_tmp[:, :], 1.0, o_tmp[:, :], ALU.bypass, ALU.mult,
            accum_out=mv2[:, 2 * ti + 1:2 * ti + 2]),
            waits=[lro[bank], last.get("t1b")], inc=True)
        e_s2 = p.op("vector", lambda v, ti=ti: v.reduce_sum(
            mv2[:, 2 * ti:2 * ti + 1], o_tmp[:, :], axis=AX), inc=True)
        e_s3 = p.op("vector", lambda v, ti=ti: v.tensor_scalar(
            mv2[:, 2 * ti:2 * ti + 1], mv2[:, 2 * ti:2 * ti + 1], 1.0 / D, None,
            ALU.mult), waits=[e_s2], inc=True)
        e_s4 = p.op("vector", lambda v, ti=ti: v.tensor_scalar(
            t1b[:, 0:1], mv2[:, 2 * ti:2 * ti + 1], mv2[:, 2 * ti:2 * ti + 1], None,
            ALU.mult), waits=[e_s3], inc=True)
        e_s5 = p.op("vector", lambda v, ti=ti: v.scalar_tensor_tensor(
            mv2[:, 2 * ti + 1:2 * ti + 2], mv2[:, 2 * ti + 1:2 * ti + 2], 1.0 / D,
            t1b[:, 0:1], ALU.mult, ALU.subtract), waits=[e_s1, e_s4], inc=True)
        e_s6 = p.op("scalar", lambda sc, ti=ti: sc.activation(
            mv2[:, 2 * ti + 1:2 * ti + 2], mv2[:, 2 * ti + 1:2 * ti + 2], AF.Ln,
            bias=ctx.ccol(LN_EPS)), waits=[e_s5], inc=True)
        e_s7 = p.op("scalar", lambda sc, ti=ti: sc.activation(
            mv2[:, 2 * ti + 1:2 * ti + 2], mv2[:, 2 * ti + 1:2 * ti + 2], AF.Exp,
            scale=-0.5), waits=[e_s6], inc=True)
        e_a1 = p.op("vector", lambda v, ti=ti: v.tensor_scalar(
            t1b[:, :], o_tmp[:, :], mv2[:, 2 * ti:2 * ti + 1],
            mv2[:, 2 * ti + 1:2 * ti + 2], ALU.subtract, ALU.mult),
            waits=[e_s7], inc=True)
        last["otmp"] = e_a1
        e_a2 = p.op("vector", lambda v, ti=ti: v.tensor_tensor(
            t1b[:, :], t1b[:, :], gb[:, 0:1024], ALU.mult), waits=[e_a1, e_rc], inc=True)
        e_a3 = p.op("vector", lambda v, ti=ti: v.tensor_tensor(
            t1b[:, :], t1b[:, :], gb[:, 1024:2048], ALU.add), waits=[e_a2], inc=True)
        e_a4 = p.op("vector", lambda v, ti=ti: v.tensor_scalar(
            t2b[:, :], ctx.resid[:, ti * D:(ti + 1) * D],
            mv[:, 2 * ti:2 * ti + 1], mv[:, 2 * ti + 1:2 * ti + 2],
            ALU.subtract, ALU.mult), waits=[e_a3, last.get("t2b")], inc=True)
        e_a5 = p.op("vector", lambda v, ti=ti: v.tensor_tensor(
            t1b[:, :], t1b[:, :], t2b[:, :], ALU.add), waits=[e_a4], inc=True)
        last["t2b"] = e_a5
        last["t1b"] = p.op("vector", lambda v, ti=ti: v.tensor_tensor(
            ctx.resid[:, ti * D:(ti + 1) * D], ctx.resid[:, ti * D:(ti + 1) * D],
            t1b[:, :], ALU.add), waits=[e_a5], inc=True)
    p.phase()


def emit_w16_stream2(ctx, w16, dram, col0, ncols, group, waits=(), nchunks=ND,
                     dst_off=0, dst_stride=None, stage=None):
    p = ctx.p
    if dst_stride is None:
        dst_stride = ncols
    if dram.dtype == BF16:
        g = group + "d"
        for ci in range(nchunks):
            p.dma(w16[:, ci * dst_stride + dst_off: ci * dst_stride + dst_off + ncols],
                  dram[ci * 128:(ci + 1) * 128, col0:col0 + ncols],
                  group=g, waits=list(waits))
        return p.dma_ev(g)
    if stage is None:
        stage = ctx.wcvt_stage
    sw = stage.shape[1] // 2
    ev_cvt = {}
    ev = None
    for ci in range(nchunks):
        par = ci % 2
        g = f"{group}{par}"
        p.dma(stage[:, par * sw: par * sw + ncols],
              dram[ci * 128:(ci + 1) * 128, col0:col0 + ncols],
              group=g, waits=[ev_cvt.get(par)] + list(waits))
        e_d = p.dma_ev(g)
        ev = p.op("gpsimd", lambda g_, ci=ci, par=par: g_.tensor_copy(
            w16[:, ci * dst_stride + dst_off: ci * dst_stride + dst_off + ncols],
            stage[:, par * sw: par * sw + ncols]),
            waits=[e_d], inc=True)
        ev_cvt[par] = ev
    return ev


def emit_transpose1(ctx, src, dst, wait=(), rows=128):
    p = ctx.p
    b0 = src.base_partition()
    ncols = src.shape[1]
    bank = ctx.psb_next[0]
    ctx.psb_next[0] ^= 1
    base = bank * 1024
    ps = ctx.psb[0:ncols, base:base + rows]
    e_pe = p.op("tensor", lambda t: t.transpose(
        ps, src, ctx.ident[b0:b0 + rows, b0:b0 + rows],
        tile_position=(b0, 0)),
        waits=list(wait) + [ctx.psb_rd[bank]], inc=True)
    e_cp = p.op("vector", lambda v: v.tensor_copy(dst, ps), waits=[e_pe], inc=True)
    ctx.psb_rd[bank] = e_cp
    return e_cp




def emit_l0attnb(ctx, qw_d, kw_d, vw_d, ow_d, decw_d, f1_d, f2_d, vec_d):
    p = ctx.p
    psf = ctx.psf

    ctx.ulast = {}
    ctx.wcvt_stage = p.scratch("acwst", [128, 2 * 1024], F32)
    mv = p.scratch("acmv", [128, 2 * NT], F32)
    mv2 = p.scratch("acmv2", [128, 2 * NT], F32)
    w16 = p.scratch("acw16", [128, ND * 256], BF16)
    xnT = p.scratch("acxnT", [128, ND * T], BF16)
    if getattr(ctx, "l1wq_free", False):
        pq = p.alias(ctx.l1_wq, "acpq", [128, 2 * T], BF16)
        pk = p.alias(ctx.l1_wq, "acpk", [128, 2 * T], BF16, delta=8192)
    else:
        pq = p.scratch("acpq", [128, 2 * T], BF16)
        pk = p.scratch("acpk", [128, 2 * T], BF16)
    v_tm = p.scratch("acv", [128, NT * 260], BF16)
    attn_tm = p.scratch("acattn", [128, NT * D], BF16)
    bwd = p.scratch("acbwd", [128, NT * 65], BF16)
    lam = p.alias(attn_tm, "aclam", [16, T], F32, delta=8 * 1024)
    lam2 = p.alias(attn_tm, "aclam2", [16, T], F32)
    LcT = p.scratch("acLcT", [128, NT * 16], F32)
    eLT = p.scratch("aceLT", [128, NT * 16], F32)
    eLrT = p.scratch("aceLrT", [128, NT * 16], F32)
    EB = p.scratch("acEB", [128, NT * 16], F32)
    Lsum = p.scratch("acLsum", [16, 16], F32)
    lsrow = p.scratch("aclsrow", [1, 256], F32)
    rows3 = p.scratch("acrows3", [1, 2 * 128], F32)
    rows316 = p.scratch("acrows316", [1, 128], BF16)
    Dsb = p.scratch("acD", [128, 128], F32)
    Asb = p.scratch("acA", [128, 128], BF16)
    qtil = p.scratch("acqt", [128, 128], BF16)
    pkTh = p.scratch("acpkT", [128, NT * 64], BF16)
    ksc = p.scratch("acksc", [128, 64], BF16)
    S_f = p.scratch("acSf", [128, 8 * 65], F32)
    S_16 = p.scratch("acS16", [128, 8 * 65], BF16)
    R_f = p.alias(S_f, "acRf", [128, 8 * 65], F32)
    R_16 = p.alias(S_16, "acR16", [128, 8 * 65], BF16)
    yf_t = p.scratch("acyf", [128, 1024], F32)
    t1_t = p.scratch("act1", [128, 1024], F32)
    xb_t = p.scratch("acxb", [128, 1024], BF16)
    gtiles = p.scratch("acgt", [128, 2 * 1024], BF16)

    OFF_QB, OFF_KB, OFF_VB, OFF_OB = 0, 1024, 2048, 3072
    OFF_DECB = 4096
    brows = p.scratch("acbrows", [1, 4112], BF16)
    bstage = p.alias(attn_tm, "acbst", [1, 4096], F32)

    attn_fm = p.alias(xnT, "acattnf", [128, ND * T], BF16)
    r1b = p.alias(attn_tm, "acr1b", [128, NT * D], BF16)
    x2T = p.alias(xnT, "acx2T", [128, ND * T], BF16)
    f1q = p.alias(pq, "acf1q", [128, ND * 1024], BF16)
    f2q = p.alias(v_tm, "acf2q", [128, ND * 1024], BF16)

    ev_ln = emit_ln_stats(ctx, lambda ti: ctx.resid[:, ti * D:(ti + 1) * D], mv,
                          sqt=p.alias(xnT, "aclnsq", [128, 1024], F32))
    e_gt = None
    for gi, rr in enumerate([4, 5]):
        p.dma(yf_t[:, :], vec_d[rr:rr + 1, 0:1024].broadcast_to([128, 1024]),
              group="acv", waits=[e_gt])
        e_gtd = p.dma_ev("acv")
        e_gt = p.op("gpsimd", lambda g, gi=gi: g.tensor_copy(
            gtiles[:, gi * 1024:(gi + 1) * 1024], yf_t[:, :]),
            waits=[e_gtd], inc=True)
    e_bc = None
    for (rr, c0, w, off) in [(0, 0, 1024, OFF_QB), (1, 0, 1024, OFF_KB),
                             (2, 0, 1024, OFF_VB), (3, 0, 1024, OFF_OB),
                             (0, 1024, 16, OFF_DECB)]:
        p.dma(bstage[0:1, 0:w], vec_d[rr:rr + 1, c0:c0 + w], group="acb",
              waits=[e_bc])
        e_d = p.dma_ev("acb")
        e_bc = p.op("gpsimd", lambda g, w=w, off=off: g.tensor_copy(
            brows[0:1, off:off + w], bstage[0:1, 0:w]), waits=[e_d], inc=True)
    p.phase()

    lastB = {}
    for ti in range(NT):
        e_y = p.op("vector", lambda v, ti=ti: v.tensor_scalar(
            yf_t[:, :], ctx.resid[:, ti * D:(ti + 1) * D],
            mv[:, 2 * ti:2 * ti + 1], mv[:, 2 * ti + 1:2 * ti + 2],
            ALU.subtract, ALU.mult), waits=[ev_ln, lastB.get("y")], inc=True)
        e_sq = p.op("vector", lambda v, ti=ti: v.scalar_tensor_tensor(
            t1_t[:, :], yf_t[:, :], 1.0, yf_t[:, :], ALU.bypass, ALU.mult,
            accum_out=mv2[:, 2 * ti + 1:2 * ti + 2]),
            waits=[e_y, lastB.get("t1")], inc=True)
        lastB["t1"] = e_sq
        e_s = p.op("vector", lambda v, ti=ti: v.reduce_sum(
            mv2[:, 2 * ti:2 * ti + 1], yf_t[:, :], axis=AX), inc=True)
        lastB["y"] = e_s
    e_m = p.op("vector", lambda v: v.tensor_scalar(
        mv2[:, 0:2 * NT:2], mv2[:, 0:2 * NT:2], 1.0 / D, None, ALU.mult),
        waits=[lastB["y"], lastB["t1"]], inc=True)
    e_m2 = p.op("vector", lambda v: v.tensor_tensor(
        t1_t[:, 0:NT], mv2[:, 0:2 * NT:2], mv2[:, 0:2 * NT:2], ALU.mult),
        waits=[e_m], inc=True)
    e_m3 = p.op("vector", lambda v: v.scalar_tensor_tensor(
        mv2[:, 1:2 * NT:2], mv2[:, 1:2 * NT:2], 1.0 / D, t1_t[:, 0:NT],
        ALU.mult, ALU.subtract), waits=[e_m2], inc=True)
    e_m4 = p.op("scalar", lambda sc: sc.activation(
        mv2[:, 1:2 * NT:2], mv2[:, 1:2 * NT:2], AF.Ln, bias=ctx.ccol(LN_EPS)),
        waits=[e_m3], inc=True)
    ev_ln2 = p.op("scalar", lambda sc: sc.activation(
        mv2[:, 1:2 * NT:2], mv2[:, 1:2 * NT:2], AF.Exp, scale=-0.5),
        waits=[e_m4], inc=True)
    p.phase()

    lastC = {}
    for ti in range(NT):
        e_y = p.op("vector", lambda v, ti=ti: v.tensor_scalar(
            yf_t[:, :], ctx.resid[:, ti * D:(ti + 1) * D],
            mv[:, 2 * ti:2 * ti + 1], mv[:, 2 * ti + 1:2 * ti + 2],
            ALU.subtract, ALU.mult), waits=[ev_ln2, lastC.get("y")], inc=True)
        e_x1 = p.op("vector", lambda v, ti=ti: v.tensor_scalar(
            t1_t[:, :], yf_t[:, :],
            mv2[:, 2 * ti:2 * ti + 1], mv2[:, 2 * ti + 1:2 * ti + 2],
            ALU.subtract, ALU.mult), waits=[e_y, lastC.get("t1")], inc=True)
        lastC["y"] = e_x1
        e_x2 = p.op("vector", lambda v, ti=ti: v.tensor_tensor(
            t1_t[:, :], t1_t[:, :], gtiles[:, 0:1024], ALU.mult),
            waits=[e_x1, e_gt], inc=True)
        e_x3 = p.op("vector", lambda v, ti=ti: v.tensor_tensor(
            xb_t[:, :], t1_t[:, :], gtiles[:, 1024:2048], ALU.add),
            waits=[e_x2, lastC.get("xb")], inc=True)
        lastC["t1"] = e_x3
        dst3d = xnT[:, :].rearrange("p (c t) -> p c t", t=T)[:, :, ti * 128:(ti + 1) * 128]
        lastC["xb"] = emit_transpose8(
            ctx, lambda ci: xb_t[:, ci * 128:(ci + 1) * 128], dst3d, wait=[e_x3])
    p.phase()
    if getattr(ctx, "dev_l0attnb", False):
        ctx.debug_dump_now("ac_xnT", xnT[:, :])

    wdec = p.scratch("acwdec", [128, ND * 16], BF16)
    decst = p.alias(ctx.wcvt_stage, "acdst", [128, ND * 16], F32)
    for ci in range(ND):
        p.dma(decst[:, ci * 16:(ci + 1) * 16], decw_d[ci * 128:(ci + 1) * 128, :],
              group="acdw")
    e_dd = p.dma_ev("acdw")
    e_dc = p.op("gpsimd", lambda g: g.tensor_copy(wdec[:, :], decst[:, :]),
                waits=[e_dd], inc=True)
    e_lam = None
    lrd = {}
    for tt in range(4):
        bank = tt % 2
        p.op("tensor", lambda t, bank=bank: t.matmul(
            psf[0:16, bank * 512:(bank + 1) * 512],
            brows[0:1, OFF_DECB:OFF_DECB + 16], ctx.ones_row[0:1, :],
            start=True, stop=False),
            waits=[e_dc, e_bc, lrd.get(bank)], inc=True)
        e_mm = None
        for ci in range(ND):
            e_mm = p.op("tensor", lambda t, tt=tt, ci=ci, bank=bank: t.matmul(
                psf[0:16, bank * 512:(bank + 1) * 512],
                wdec[:, ci * 16:(ci + 1) * 16],
                xnT[:, ci * T + tt * 512: ci * T + (tt + 1) * 512],
                start=False, stop=(ci == ND - 1)), inc=True)
        e_lam = p.op("scalar", lambda sc, tt=tt, bank=bank: sc.activation(
            lam[:, tt * 512:(tt + 1) * 512], psf[0:16, bank * 512:(bank + 1) * 512],
            AF.Sigmoid), waits=[e_mm], inc=True)
        lrd[bank] = e_lam
    e_l1 = p.op("vector", lambda v: v.tensor_single_scalar(
        lam[:, :], lam[:, :], 1e-8, ALU.max), waits=[e_lam], inc=True)
    e_l2 = p.op("scalar", lambda sc: sc.activation(
        lam[:, :], lam[:, :], AF.Ln), waits=[e_l1], inc=True)
    e_sc = None
    e_ls = None
    for c in range(NT):
        e_ls = p.op("vector", lambda v, c=c: v.reduce_sum(
            Lsum[:, c:c + 1], lam[:, c * 128:(c + 1) * 128], axis=AX),
            waits=[e_l2], inc=True)
        e_sc = p.op("vector", lambda v, c=c: v.tensor_tensor_scan(
            lam2[:, c * 128:(c + 1) * 128], lam[:, c * 128:(c + 1) * 128],
            lam[:, c * 128:(c + 1) * 128], 0.0, ALU.add, ALU.bypass),
            waits=[e_ls], inc=True)
    e_lct = None
    for c in range(NT):
        e_lct = emit_transpose1_f32(ctx, lam2[0:16, c * 128:(c + 1) * 128],
                                    LcT[:, c * 16:(c + 1) * 16],
                                    wait=[e_sc], rows=16)
    e_lr = None
    for c in range(NT):
        e_lr = p.op("tensor", lambda t, c=c: t.matmul(
            psf[0:1, 2 * 512 + c * 16: 2 * 512 + (c + 1) * 16],
            Lsum[:, c:c + 1], ctx.identf[0:16, 0:16], start=True, stop=True),
            waits=[e_ls, e_sc, ctx.ulast.get("t1f32")], inc=True)
    e_lr2 = p.op("vector", lambda v: v.tensor_copy(
        lsrow[0:1, :], psf[0:1, 2 * 512: 2 * 512 + 256]), waits=[e_lr], inc=True)
    e_lb = p.op("tensor", lambda t: t.matmul(
        psf[:, 2 * 512 + 256: 2 * 512 + 512],
        ctx.onesf[0:1, 0:128], lsrow[0:1, 0:256],
        start=True, stop=True), waits=[e_lr2], inc=True)
    e_lsb = p.op("vector", lambda v: v.tensor_copy(
        EB[:, :], psf[:, 2 * 512 + 256: 2 * 512 + 512]), waits=[e_lb], inc=True)
    e_elr1 = p.op("vector", lambda v: v.tensor_tensor(
        eLrT[:, :], EB[:, :], LcT[:, :], ALU.subtract), waits=[e_lsb, e_lct], inc=True)
    e_elr = p.op("scalar", lambda sc: sc.activation(
        eLrT[:, :], eLrT[:, :], AF.Exp), waits=[e_elr1], inc=True)
    e_eb = p.op("scalar", lambda sc: sc.activation(
        EB[:, :], EB[:, :], AF.Exp), waits=[e_elr1], inc=True)
    e_elt = p.op("scalar", lambda sc: sc.activation(
        eLT[:, :], LcT[:, :], AF.Exp), waits=[e_lct], inc=True)
    p.phase()
    if getattr(ctx, "dev_l0attnb", False):
        ctx.debug_dump_now("ac_lam2", lam2[:, :])
        ctx.debug_dump_now("ac_LcT", LcT[:, :])
        ctx.debug_dump_now("ac_EB", EB[:, :])

    def build_escale_row(h, c, use_eLr, waits):
        col = c * 16 + h
        e1 = p.op("tensor", lambda t, col=col: t.matmul(
            psf[0:1, 2 * 512: 2 * 512 + 128],
            LcT[:, col:col + 1], ctx.identf[:, :], start=True, stop=True),
            waits=list(waits) + [ctx.ulast.get("rows3")], inc=True)
        e2 = p.op("vector", lambda v: v.tensor_copy(
            rows3[0:1, 0:128], psf[0:1, 2 * 512: 2 * 512 + 128]),
            waits=[e1], inc=True)
        if use_eLr:
            e3 = p.op("vector", lambda v, col=col: v.tensor_scalar(
                rows3[0:1, 128:256], rows3[0:1, 0:128],
                lsrow[0:1, col:col + 1], None, ALU.subtract),
                waits=[e2], inc=True)
            e4 = p.op("scalar", lambda sc: sc.activation(
                rows3[0:1, 128:256], rows3[0:1, 128:256], AF.Exp, scale=-1.0),
                waits=[e3], inc=True)
        else:
            e4 = p.op("scalar", lambda sc: sc.activation(
                rows3[0:1, 128:256], rows3[0:1, 0:128], AF.Exp),
                waits=[e2], inc=True)
        e5 = p.op("vector", lambda v: v.tensor_copy(
            rows316[0:1, 0:128], rows3[0:1, 128:256]), waits=[e4], inc=True)
        ctx.ulast["rows3"] = e5
        return e5

    def build_lc_row(h, c, waits):
        col = c * 16 + h
        e1 = p.op("tensor", lambda t, col=col: t.matmul(
            psf[0:1, 2 * 512: 2 * 512 + 128],
            LcT[:, col:col + 1], ctx.identf[:, :], start=True, stop=True),
            waits=list(waits) + [ctx.ulast.get("rows3")], inc=True)
        e2 = p.op("vector", lambda v: v.tensor_copy(
            rows3[0:1, 0:128], psf[0:1, 2 * 512: 2 * 512 + 128]),
            waits=[e1], inc=True)
        ctx.ulast["rows3"] = e2
        return e2

    def build_qscaled(h, c, b0, col_q, waits):
        e1 = p.op("tensor", lambda t, b0=b0: t.matmul(
            psf[b0:b0 + 64, 3 * 512: 3 * 512 + 128],
            ctx.one_row[0:1, 0:64], rows316[0:1, 0:128],
            start=True, stop=True, tile_position=(0, b0)),
            waits=list(waits) + [ctx.ulast.get("qtil")], inc=True)
        e2 = p.op("vector", lambda v, b0=b0, col_q=col_q, c=c: v.tensor_tensor(
            qtil[b0:b0 + 64, 0:128],
            pq[b0:b0 + 64, col_q * T + c * 128: col_q * T + (c + 1) * 128],
            psf[b0:b0 + 64, 3 * 512: 3 * 512 + 128], ALU.mult),
            waits=[e1], inc=True)
        ctx.ulast["qtil"] = e2
        return e2

    for hg in range(4):
        ev_wq = emit_w16_stream2(ctx, w16, qw_d, hg * 256, 256, "acw",
                                 stage=ctx.wcvt_stage,
                                 waits=[ctx.ulast.get("wdone")])

        def proj_fm_phi(dst, w_ev, boff):
            lr = {}
            e_p3 = None
            for col in range(2):
                for tt in range(4):
                    bank = (col * 4 + tt) % 4
                    p.op("tensor", lambda t, col=col, bank=bank, boff=boff, hg=hg: t.matmul(
                        psf[:, bank * 512:(bank + 1) * 512],
                        brows[0:1, boff + hg * 256 + col * 128:
                              boff + hg * 256 + (col + 1) * 128],
                        ctx.ones_row[0:1, :], start=True, stop=False),
                        waits=[w_ev, e_bc, lr.get(bank)], inc=True)
                    e_mm = None
                    for ci in range(ND):
                        e_mm = p.op("tensor", lambda t, col=col, tt=tt, ci=ci, bank=bank: t.matmul(
                            psf[:, bank * 512:(bank + 1) * 512],
                            w16[:, ci * 256 + col * 128: ci * 256 + (col + 1) * 128],
                            xnT[:, ci * T + tt * 512: ci * T + (tt + 1) * 512],
                            start=False, stop=(ci == ND - 1)), inc=True)
                    e_p1 = p.op("vector", lambda v, bank=bank: v.tensor_single_scalar(
                        t1_t[:, 0:512], psf[:, bank * 512:(bank + 1) * 512], 0.0,
                        ALU.min), waits=[e_mm, lr.get("t1")], inc=True)
                    e_p2 = p.op("scalar", lambda sc: sc.activation(
                        t1_t[:, 0:512], t1_t[:, 0:512], AF.Exp), waits=[e_p1], inc=True)
                    e_p3 = p.op("vector", lambda v, col=col, tt=tt, bank=bank, dst=dst: v.scalar_tensor_tensor(
                        dst[:, col * T + tt * 512: col * T + (tt + 1) * 512],
                        psf[:, bank * 512:(bank + 1) * 512], 0.0, t1_t[:, 0:512],
                        ALU.max, ALU.add), waits=[e_p2], inc=True)
                    lr["t1"] = e_p3
                    lr[bank] = e_p3
            return e_p3

        e_q = proj_fm_phi(pq, ev_wq, OFF_QB)
        ev_wk = emit_w16_stream2(ctx, w16, kw_d, hg * 256, 256, "acw",
                                 stage=ctx.wcvt_stage, waits=[e_q])
        e_k = proj_fm_phi(pk, ev_wk, OFF_KB)
        ev_wv = emit_w16_stream2(ctx, w16, vw_d, hg * 256, 256, "acw",
                                 stage=ctx.wcvt_stage, waits=[e_k])
        e_vo = p.op("vector", lambda v: v.memset(
            v_tm[:, :].rearrange("p (ti c) -> p ti c", c=260)[:, :, 64:260:65], 1.0),
            waits=[ctx.ulast.get("vdone")], inc=True)
        lrv = {}
        for ti in range(NT):
            bank = 4 + ti % 2
            p.op("tensor", lambda t, bank=bank, hg=hg: t.matmul(
                psf[:, bank * 512: bank * 512 + 256],
                ctx.one_row[:, :],
                brows[0:1, OFF_VB + hg * 256: OFF_VB + (hg + 1) * 256],
                start=True, stop=False),
                waits=[ev_wv, e_bc, e_vo, lrv.get(bank)], inc=True)
            e_mm = None
            for ci in range(ND):
                e_mm = p.op("tensor", lambda t, ti=ti, ci=ci, bank=bank: t.matmul(
                    psf[:, bank * 512: bank * 512 + 256],
                    xnT[:, ci * T + ti * 128: ci * T + (ti + 1) * 128],
                    w16[:, ci * 256:(ci + 1) * 256],
                    start=False, stop=(ci == ND - 1)), inc=True)
            e_ev = None
            for hl in range(4):
                e_ev = p.op("vector", lambda v, ti=ti, hl=hl, bank=bank: v.tensor_copy(
                    v_tm[:, ti * 260 + hl * 65: ti * 260 + hl * 65 + 64],
                    psf[:, bank * 512 + hl * 64: bank * 512 + hl * 64 + 64]),
                    waits=[e_mm], inc=True)
            lrv[bank] = e_ev
        p.phase()

        for hl in range(4):
            h = hg * 4 + hl
            b0 = (hl % 2) * 64
            col_q = hl // 2
            s_sl = slice(hl * 65, hl * 65 + 65)
            e_pkT = None
            for c in range(NT):
                e_pkT = emit_transpose1(
                    ctx, pk[b0:b0 + 64, col_q * T + c * 128: col_q * T + (c + 1) * 128],
                    pkTh[:, c * 64:(c + 1) * 64], wait=[], rows=64)
            e_R0 = p.op("vector", lambda v, b0=b0, s_sl=s_sl: v.memset(
                R_f[b0:b0 + 64, s_sl], 0.0), waits=[ctx.ulast.get("states")], inc=True)
            e_R16 = p.op("vector", lambda v, b0=b0, s_sl=s_sl: v.tensor_copy(
                R_16[b0:b0 + 64, s_sl], R_f[b0:b0 + 64, s_sl]), waits=[e_R0], inc=True)
            for c in range(NT - 1, -1, -1):
                col = c * 16 + h
                e_ob = p.op("tensor", lambda t, b0=b0, s_sl=s_sl, c=c, col_q=col_q: t.matmul(
                    psf[:, 4 * 512: 4 * 512 + 65],
                    pq[b0:b0 + 64, col_q * T + c * 128: col_q * T + (c + 1) * 128],
                    R_16[b0:b0 + 64, s_sl],
                    start=True, stop=True, tile_position=(b0, 0)),
                    waits=[e_R16, ctx.ulast.get("ob_rd"),
                           ctx.ulast.get("of_rd")], inc=True)
                ctx.ulast["ob_rd"] = p.op("vector", lambda v, c=c, col=col: v.tensor_scalar(
                    bwd[:, c * 65:(c + 1) * 65],
                    psf[:, 4 * 512: 4 * 512 + 65],
                    eLrT[:, col:col + 1], None, ALU.mult),
                    waits=[e_ob, e_elr], inc=True)
                if c > 0:
                    e_kc = p.op("vector", lambda v, c=c, col=col: v.tensor_scalar(
                        ksc[:, 0:64], pkTh[:, c * 64:(c + 1) * 64],
                        eLT[:, col:col + 1], None, ALU.mult),
                        waits=[e_pkT, e_elt, ctx.ulast.get("ksc")], inc=True)
                    e_rd = p.op("tensor", lambda t, c=c, b0=b0, hl=hl: t.matmul(
                        psf[b0:b0 + 64, 5 * 512: 5 * 512 + 65],
                        ksc[:, 0:64],
                        v_tm[:, c * 260 + hl * 65: c * 260 + (hl + 1) * 65],
                        start=True, stop=True, tile_position=(0, b0)),
                        waits=[e_kc, ctx.ulast.get("rd_rd"),
                               ctx.ulast.get("sd_rd")], inc=True)
                    ctx.ulast["ksc"] = e_rd
                    e_ru = p.op("vector", lambda v, b0=b0, s_sl=s_sl, col=col: v.scalar_tensor_tensor(
                        R_f[b0:b0 + 64, s_sl], R_f[b0:b0 + 64, s_sl],
                        EB[b0:b0 + 64, col:col + 1],
                        psf[b0:b0 + 64, 5 * 512: 5 * 512 + 65], ALU.mult, ALU.add),
                        waits=[e_rd, e_eb, e_R16], inc=True)
                    ctx.ulast["rd_rd"] = e_ru
                    e_R16 = p.op("vector", lambda v, b0=b0, s_sl=s_sl: v.tensor_copy(
                        R_16[b0:b0 + 64, s_sl], R_f[b0:b0 + 64, s_sl]),
                        waits=[e_ru], inc=True)
            e_S0 = p.op("vector", lambda v, b0=b0, s_sl=s_sl: v.memset(
                S_f[b0:b0 + 64, s_sl], 0.0), waits=[ctx.ulast.get("ob_rd"),
                                                    ctx.ulast.get("rd_rd")], inc=True)
            e_S16 = p.op("vector", lambda v, b0=b0, s_sl=s_sl: v.tensor_copy(
                S_16[b0:b0 + 64, s_sl], S_f[b0:b0 + 64, s_sl]), waits=[e_S0], inc=True)
            for c in range(NT):
                col = c * 16 + h
                e_qk = p.op("tensor", lambda t, c=c, b0=b0, col_q=col_q: t.matmul(
                    psf[:, 0 * 512: 0 * 512 + 128],
                    pk[b0:b0 + 64, col_q * T + c * 128: col_q * T + (c + 1) * 128],
                    pq[b0:b0 + 64, col_q * T + c * 128: col_q * T + (c + 1) * 128],
                    start=True, stop=True, tile_position=(b0, 0)),
                    waits=[ctx.ulast.get("qk_rd")], inc=True)
                e_rows = build_lc_row(h, c, [e_lct, e_lr2])
                e_db = p.op("tensor", lambda t: t.matmul(
                    psf[:, 1 * 512: 1 * 512 + 128],
                    ctx.onesf[0:1, 0:128], rows3[0:1, 0:128],
                    start=True, stop=True), waits=[e_rows, ctx.ulast.get("db_rd")],
                    inc=True)
                e_d1 = p.op("vector", lambda v, col=col: v.tensor_scalar(
                    Dsb[:, :], psf[:, 1 * 512: 1 * 512 + 128],
                    LcT[:, col:col + 1], None, ALU.subtract),
                    waits=[e_db], inc=True)
                ctx.ulast["db_rd"] = e_d1
                e_d1 = p.op("vector", lambda v: v.scalar_tensor_tensor(
                    Dsb[:, :], Dsb[:, :], -1.0, Dsb[:, :], ALU.mult, ALU.max),
                    waits=[e_d1], inc=True)
                e_d2 = p.op("scalar", lambda sc: sc.activation(
                    Dsb[:, :], Dsb[:, :], AF.Exp, scale=-1.0), waits=[e_d1], inc=True)
                e_A = p.op("vector", lambda v: v.tensor_tensor(
                    Asb[:, :], psf[:, 0 * 512: 0 * 512 + 128], Dsb[:, :], ALU.mult),
                    waits=[e_d2, e_qk], inc=True)
                ctx.ulast["qk_rd"] = e_A
                e_o1 = p.op("tensor", lambda t, c=c, hl=hl: t.matmul(
                    psf[:, 4 * 512: 4 * 512 + 65],
                    Asb[:, :], v_tm[:, c * 260 + hl * 65: c * 260 + (hl + 1) * 65],
                    start=True, stop=True),
                    waits=[e_A, ctx.ulast.get("of_rd"), ctx.ulast.get("ob_rd")],
                    inc=True)
                e_o2 = p.op("tensor", lambda t, b0=b0, s_sl=s_sl, c=c,
                            col_q=col_q: t.matmul(
                    psf[:, 4 * 512 + 65: 4 * 512 + 130],
                    pq[b0:b0 + 64, col_q * T + c * 128: col_q * T + (c + 1) * 128],
                    S_16[b0:b0 + 64, s_sl],
                    start=True, stop=True, tile_position=(b0, 0)),
                    waits=[e_S16], inc=True)
                e_x1 = p.op("vector", lambda v, col=col: v.tensor_scalar(
                    t1_t[:, 65:130], psf[:, 4 * 512 + 65: 4 * 512 + 130],
                    eLT[:, col:col + 1], None, ALU.mult),
                    waits=[e_o2, e_elt, ctx.ulast.get("t1l")], inc=True)
                e_e1 = p.op("vector", lambda v: v.tensor_tensor(
                    t1_t[:, 0:65], psf[:, 4 * 512: 4 * 512 + 65],
                    t1_t[:, 65:130], ALU.add),
                    waits=[e_o1, e_x1], inc=True)
                ctx.ulast["of_rd"] = e_e1
                e_e1 = p.op("vector", lambda v, c=c: v.tensor_tensor(
                    t1_t[:, 0:65], t1_t[:, 0:65],
                    bwd[:, c * 65:(c + 1) * 65], ALU.add),
                    waits=[e_e1], inc=True)
                e_e2 = p.op("vector", lambda v: v.tensor_single_scalar(
                    t1_t[:, 64:65], t1_t[:, 64:65], 1e-6, ALU.max), waits=[e_e1], inc=True)
                e_e3 = p.op("vector", lambda v: v.reciprocal(
                    t1_t[:, 64:65], t1_t[:, 64:65]), waits=[e_e2], inc=True)
                e_e4 = p.op("vector", lambda v, c=c, h=h: v.tensor_scalar(
                    attn_tm[:, c * D + h * 64: c * D + (h + 1) * 64],
                    t1_t[:, 0:64], t1_t[:, 64:65], None, ALU.mult),
                    waits=[e_e3], inc=True)
                ctx.ulast["t1l"] = e_e4
                e_kh = p.op("vector", lambda v, c=c, col=col: v.tensor_scalar(
                    ksc[:, 0:64], pkTh[:, c * 64:(c + 1) * 64],
                    eLrT[:, col:col + 1], None, ALU.mult),
                    waits=[e_pkT, e_elr, ctx.ulast.get("ksc")], inc=True)
                e_sd = p.op("tensor", lambda t, c=c, b0=b0, hl=hl: t.matmul(
                    psf[b0:b0 + 64, 5 * 512: 5 * 512 + 65],
                    ksc[:, 0:64],
                    v_tm[:, c * 260 + hl * 65: c * 260 + (hl + 1) * 65],
                    start=True, stop=True, tile_position=(0, b0)),
                    waits=[e_kh, ctx.ulast.get("sd_rd"),
                           ctx.ulast.get("rd_rd")], inc=True)
                ctx.ulast["ksc"] = e_sd
                e_su = p.op("vector", lambda v, b0=b0, s_sl=s_sl, col=col: v.scalar_tensor_tensor(
                    S_f[b0:b0 + 64, s_sl], S_f[b0:b0 + 64, s_sl],
                    EB[b0:b0 + 64, col:col + 1],
                    psf[b0:b0 + 64, 5 * 512: 5 * 512 + 65], ALU.mult, ALU.add),
                    waits=[e_sd, e_eb, e_S16], inc=True)
                ctx.ulast["sd_rd"] = e_su
                e_S16 = p.op("vector", lambda v, b0=b0, s_sl=s_sl: v.tensor_copy(
                    S_16[b0:b0 + 64, s_sl], S_f[b0:b0 + 64, s_sl]), waits=[e_su], inc=True)
            ctx.ulast["states"] = e_S16
        ctx.ulast["wdone"] = e_q
        ctx.ulast["vdone"] = ctx.ulast.get("sd_rd")
        p.phase()
    p.phase()
    if getattr(ctx, "dev_l0attnb", False):
        ctx.debug_dump_now("ac_attn", attn_tm[:, :])

    w16o = p.alias(pq, "acw16o", [128, ND * 1024], BF16)
    ev_wo = emit_w16_stream(ctx, w16o, ow_d, 0, 1024, "acwo")
    e_af = None
    for ti in range(NT):
        dst3d = attn_fm[:, :].rearrange("p (c t) -> p c t", t=T)[:, :, ti * 128:(ti + 1) * 128]
        e_af = emit_transpose8(
            ctx, lambda ci, ti=ti: attn_tm[:, ti * D + ci * 128: ti * D + (ci + 1) * 128],
            dst3d, wait=[])
    p.phase()
    lastF = {}
    for ti in range(NT):
        par = ti % 2
        for oc in range(2):
            bank = par * 2 + oc
            p.op("tensor", lambda t, ti=ti, oc=oc, bank=bank: t.matmul(
                psf[:, bank * 512:(bank + 1) * 512],
                ctx.one_row[:, :], brows[0:1, OFF_OB + oc * 512: OFF_OB + (oc + 1) * 512],
                start=True, stop=False),
                waits=[ev_wo, e_af, lastF.get(bank)], inc=True)
            e_mm = None
            for ci in range(ND):
                e_mm = p.op("tensor", lambda t, ti=ti, oc=oc, ci=ci, bank=bank: t.matmul(
                    psf[:, bank * 512:(bank + 1) * 512],
                    attn_fm[:, ci * T + ti * 128: ci * T + (ti + 1) * 128],
                    w16o[:, ci * 1024 + oc * 512: ci * 1024 + (oc + 1) * 512],
                    start=False, stop=(ci == ND - 1)), inc=True)
            lastF[("mm", oc)] = e_mm
        e_y = p.op("vector", lambda v, ti=ti: v.tensor_scalar(
            yf_t[:, :], ctx.resid[:, ti * D:(ti + 1) * D],
            mv[:, 2 * ti:2 * ti + 1], mv[:, 2 * ti + 1:2 * ti + 2],
            ALU.subtract, ALU.mult), waits=[lastF.get("y"), lastF.get("yf")],
            inc=True)
        e_r1 = None
        for oc in range(2):
            bank = par * 2 + oc
            e_r1 = p.op("vector", lambda v, oc=oc, bank=bank: v.tensor_tensor(
                t1_t[:, oc * 512:(oc + 1) * 512], yf_t[:, oc * 512:(oc + 1) * 512],
                psf[:, bank * 512:(bank + 1) * 512], ALU.add),
                waits=[lastF[("mm", oc)], e_y, lastF.get("t1")], inc=True)
            lastF[bank] = e_r1
        lastF["y"] = e_r1
        e_ra = p.op("vector", lambda v, ti=ti: v.tensor_tensor(
            ctx.resid[:, ti * D:(ti + 1) * D], ctx.resid[:, ti * D:(ti + 1) * D],
            t1_t[:, :], ALU.add), waits=[e_r1], inc=True)
        e_sq = p.op("vector", lambda v, ti=ti: v.scalar_tensor_tensor(
            yf_t[:, :], t1_t[:, :], 1.0, t1_t[:, :], ALU.bypass, ALU.mult,
            accum_out=mv2[:, 2 * ti + 1:2 * ti + 2]),
            waits=[e_ra], inc=True)
        lastF["yf"] = e_sq
        e_s = p.op("vector", lambda v, ti=ti: v.reduce_sum(
            mv2[:, 2 * ti:2 * ti + 1], t1_t[:, :], axis=AX), inc=True)
        e_rb = p.op("vector", lambda v, ti=ti: v.tensor_copy(
            r1b[:, ti * D:(ti + 1) * D], t1_t[:, :]), waits=[e_s], inc=True)
        lastF["t1"] = e_rb
    e_m = p.op("vector", lambda v: v.tensor_scalar(
        mv2[:, 0:2 * NT:2], mv2[:, 0:2 * NT:2], 1.0 / D, None, ALU.mult),
        waits=[lastF["t1"], lastF["yf"]], inc=True)
    e_m2 = p.op("vector", lambda v: v.tensor_tensor(
        t1_t[:, 0:NT], mv2[:, 0:2 * NT:2], mv2[:, 0:2 * NT:2], ALU.mult),
        waits=[e_m], inc=True)
    e_m3 = p.op("vector", lambda v: v.scalar_tensor_tensor(
        mv2[:, 1:2 * NT:2], mv2[:, 1:2 * NT:2], 1.0 / D, t1_t[:, 0:NT],
        ALU.mult, ALU.subtract), waits=[e_m2], inc=True)
    e_m4 = p.op("scalar", lambda sc: sc.activation(
        mv2[:, 1:2 * NT:2], mv2[:, 1:2 * NT:2], AF.Ln, bias=ctx.ccol(LN_EPS)),
        waits=[e_m3], inc=True)
    ev_lnf = p.op("scalar", lambda sc: sc.activation(
        mv2[:, 1:2 * NT:2], mv2[:, 1:2 * NT:2], AF.Exp, scale=-0.5),
        waits=[e_m4], inc=True)
    e_gt3 = None
    for gi, rr in enumerate([6, 7]):
        p.dma(yf_t[:, :], vec_d[rr:rr + 1, 0:1024].broadcast_to([128, 1024]),
              group="acv2", waits=[e_gt3, lastF.get("yf"), lastF.get("y")])
        e_gtd = p.dma_ev("acv2")
        e_gt3 = p.op("gpsimd", lambda g, gi=gi: g.tensor_copy(
            gtiles[:, gi * 1024:(gi + 1) * 1024], yf_t[:, :]),
            waits=[e_gtd], inc=True)
    p.phase()

    lastG = {}
    for ti in range(NT):
        e_x1 = p.op("vector", lambda v, ti=ti: v.tensor_scalar(
            t1_t[:, :], r1b[:, ti * D:(ti + 1) * D],
            mv2[:, 2 * ti:2 * ti + 1], mv2[:, 2 * ti + 1:2 * ti + 2],
            ALU.subtract, ALU.mult), waits=[ev_lnf, lastG.get("t1")], inc=True)
        e_x2 = p.op("vector", lambda v, ti=ti: v.tensor_tensor(
            t1_t[:, :], t1_t[:, :], gtiles[:, 0:1024], ALU.mult),
            waits=[e_x1, e_gt3], inc=True)
        e_x3 = p.op("vector", lambda v, ti=ti: v.tensor_tensor(
            xb_t[:, :], t1_t[:, :], gtiles[:, 1024:2048], ALU.add),
            waits=[e_x2, lastG.get("xb")], inc=True)
        lastG["t1"] = e_x3
        dst3d = x2T[:, :].rearrange("p (c t) -> p c t", t=T)[:, :, ti * 128:(ti + 1) * 128]
        lastG["xb"] = emit_transpose8(
            ctx, lambda ci: xb_t[:, ci * 128:(ci + 1) * 128], dst3d, wait=[e_x3])
    if getattr(ctx, "dev_l0attnb", False):
        ctx.debug_dump_now("ac_x2T", x2T[:, :], waits=[lastG.get("xb")])
    f1brows = p.alias(rows316, "acf1b", [1, 2048], BF16)
    fbstage = p.alias(LcT, "acfbst", [1, 1024], F32)
    p.dma(fbstage[0:1, 0:1024], vec_d[1:2, 1024:2048], group="acb3")
    e_f2bd = p.dma_ev("acb3")
    e_f1b = p.op("gpsimd", lambda g: g.tensor_copy(
        f1brows[0:1, 1024:2048], fbstage[0:1, 0:1024]), waits=[e_f2bd], inc=True)
    p.phase()
    if getattr(ctx, "dev_l0attnb", False):
        ctx.debug_dump_now("ac_r1b", r1b[:, :])
        ctx.debug_dump_now("ac_mv2", mv2[:, :])
        ctx.debug_dump_now("ac_gt", gtiles[:, :])

    h_fm = p.alias(brows, "ach2", [128, 8 * 512], BF16)
    for q4 in range(4):
        p.dma(fbstage[0:1, 0:1024], vec_d[8:9, q4 * 1024:(q4 + 1) * 1024],
              group="acb2", waits=[ctx.ulast.get("f1brd")])
        e_f1bd = p.dma_ev("acb2")
        e_f1b = p.op("gpsimd", lambda g: g.tensor_copy(
            f1brows[0:1, 0:1024], fbstage[0:1, 0:1024]), waits=[e_f1bd], inc=True)
        for ci in range(ND):
            p.dma(f1q[:, ci * 1024:(ci + 1) * 1024],
                  f1_d[ci * 128:(ci + 1) * 128, q4 * 1024:(q4 + 1) * 1024],
                  group="acf1", waits=[ctx.ulast.get("f1rd")])
        ev_f1 = p.dma_ev("acf1")
        ev_f2 = None
        for fc in range(8):
            p.dma(f2q[:, fc * 1024:(fc + 1) * 1024],
                  f2_d[q4 * 1024 + fc * 128: q4 * 1024 + (fc + 1) * 128, :],
                  group="acf2", waits=[ctx.ulast.get("f2rd")])
        ev_f2 = p.dma_ev("acf2")
        lastH = {}
        for tt in range(4):
            e_h = None
            for fc in range(8):
                bank = fc % 2
                e_hb = p.op("tensor", lambda t, fc=fc, bank=bank: t.matmul(
                    psf[:, bank * 512:(bank + 1) * 512],
                    f1brows[0:1, fc * 128:(fc + 1) * 128],
                    ctx.ones_row[0:1, :], start=True, stop=False),
                    waits=[ev_f1, e_f1b, lastH.get(("hb", bank))], inc=True)
                ctx.ulast["f1brd"] = e_hb
                e_mm = None
                for ci in range(ND):
                    e_mm = p.op("tensor", lambda t, fc=fc, ci=ci, tt=tt, bank=bank: t.matmul(
                        psf[:, bank * 512:(bank + 1) * 512],
                        f1q[:, ci * 1024 + fc * 128: ci * 1024 + (fc + 1) * 128],
                        x2T[:, ci * T + tt * 512: ci * T + (tt + 1) * 512],
                        start=False, stop=(ci == ND - 1)), inc=True)
                e_h = p.op("scalar", lambda sc, fc=fc, bank=bank: sc.activation(
                    h_fm[:, fc * 512:(fc + 1) * 512],
                    psf[:, bank * 512:(bank + 1) * 512], AF.Gelu),
                    waits=[e_mm], inc=True)
                lastH[("hb", bank)] = e_h
            for tsub in range(4):
                ti = tt * 4 + tsub
                for oc in range(2):
                    bank = 2 + (tsub % 2) * 2 + oc
                    if q4 == 0:
                        p.op("tensor", lambda t, oc=oc, bank=bank: t.matmul(
                            psf[:, bank * 512:(bank + 1) * 512],
                            ctx.one_row[:, :],
                            f1brows[0:1, 1024 + oc * 512: 1024 + (oc + 1) * 512],
                            start=True, stop=False),
                            waits=[e_f1b, lastH.get(bank)], inc=True)
                    e_mm = None
                    for fc in range(8):
                        e_mm = p.op("tensor", lambda t, fc=fc, oc=oc, tsub=tsub,
                                    bank=bank, first=(q4 > 0 and fc == 0): t.matmul(
                            psf[:, bank * 512:(bank + 1) * 512],
                            h_fm[:, fc * 512 + tsub * 128: fc * 512 + (tsub + 1) * 128],
                            f2q[:, fc * 1024 + oc * 512: fc * 1024 + (oc + 1) * 512],
                            start=first, stop=(fc == 7)),
                            waits=([e_h, ev_f2, lastH.get(bank)] if fc == 0 else []),
                            inc=True)
                    lastH[bank] = p.op("vector", lambda v, ti=ti, oc=oc, bank=bank: v.tensor_tensor(
                        ctx.resid[:, ti * D + oc * 512: ti * D + (oc + 1) * 512],
                        ctx.resid[:, ti * D + oc * 512: ti * D + (oc + 1) * 512],
                        psf[:, bank * 512:(bank + 1) * 512], ALU.add),
                        waits=[e_mm], inc=True)
                    ctx.ulast["f1rd"] = e_h
                    ctx.ulast["f2rd"] = lastH[bank]
        if getattr(ctx, "dev_l0attnb", False):
            ctx.debug_dump_now(f"ac_hfm_{q4}", h_fm[:, :],
                               waits=[lastH.get(("hb", 0)), lastH.get(("hb", 1)),
                                      lastH.get(2), lastH.get(3), lastH.get(4),
                                      lastH.get(5)])
            ctx.debug_dump_now(f"ac_f1q_{q4}", f1q[:, :], waits=[ev_f1])
            ctx.debug_dump_now(f"ac_f2q_{q4}", f2q[:, :], waits=[ev_f2])
        p.phase()





def emit_l1attnb(ctx, wq_d, wk_d, wv_d, wo_d, vec_d, vne_d, cos_d, sin_d,
                 j2t_d, epscol_d):
    p = ctx.p
    psf = ctx.psf
    SC2 = math.pi / math.sqrt(3.0) / 2.0

    ctx.ulast = {}
    ctx.wcvt_stage = p.scratch("adwst", [128, 2 * 1024], F32)
    mv = p.scratch("admv", [128, 2 * NT], F32)
    ynT = p.scratch("adynT", [128, ND * T], BF16)
    k_fm = p.scratch("adk", [128, 2 * T], BF16)
    v_tm = p.scratch("adv", [128, NT * 260], BF16)
    q_fm = p.scratch("adq", [128, 2 * T], BF16)
    attn_n = p.scratch("adattn", [128, NT * 256], BF16)
    attn_fm = p.scratch("adattnf", [128, 2 * T], BF16)
    w16 = p.scratch("adw16", [128, ND * 256], BF16)
    w16o = p.scratch("adw16o", [128, 2 * 1024], BF16)
    cos2 = p.scratch("adcos", [128, T], BF16)
    sin2 = p.scratch("adsin", [128, T], BF16)
    j2t = p.scratch("adj2t", [128, 128], BF16)
    kss = p.scratch("adkss", [128, NT * 4], F32)
    sinkc = p.scratch("adsinkc", [128, 64], F32)
    sinkf = p.scratch("adsinkf", [1, 64], F32)
    vnq = p.scratch("advnq", [1, 4 * 65], BF16)
    epsc = p.scratch("adepsc", [1, 65], BF16)
    brow = p.scratch("adbrow", [1, 2112], BF16)
    OFF_BK, OFF_BV, OFF_SK = 0, 1024, 2048
    bqrow = p.scratch("adbqrow", [1, 256], BF16)
    borow = p.scratch("adborow", [1, 1024], BF16)
    zb = p.scratch("adzb", [128, 1024], BF16)
    tb = p.scratch("adtb", [128, 1024], BF16)
    wsc = p.alias(ctx.wcvt_stage, "adzb2", [128, 4 * 1024], BF16)
    zbs = (zb[:, 0:1024], wsc[:, 0:1024])
    tbs = (tb[:, 0:1024], wsc[:, 1024:2048])
    um = p.scratch("adum", [128, 2 * 1024], BF16)
    ums = (um[:, 0:2048], wsc[:, 2048:4096])
    ctx_sb = p.scratch("adctx", [65, 512], BF16)
    qtmp = p.scratch("adqtmp", [128, 2 * 512], BF16)
    ctn = p.scratch("adctn", [128, 65], BF16)
    kt_t = p.scratch("adktt", [128, 128], BF16)
    t1_t = p.scratch("adt1", [128, 128], F32)
    xb_t = p.scratch("adxb", [128, 1024], BF16)
    bstage = p.alias(ctx.wcvt_stage, "adbst", [1, 1024], F32)

    ev_ln = emit_ln_stats(ctx, lambda ti: ctx.resid[:, ti * D:(ti + 1) * D], mv,
                          sqt=p.alias(ynT, "adlnsq", [128, 1024], F32))
    e_bc = None
    for (rr, c0, w, off) in [(1, 0, 1024, OFF_BK), (1, 1024, 1024, OFF_BV)]:
        p.dma(bstage[0:1, 0:w], vec_d[rr:rr + 1, c0:c0 + w], group="adb",
              waits=[e_bc])
        e_d = p.dma_ev("adb")
        e_bc = p.op("gpsimd", lambda g, w=w, off=off: g.tensor_copy(
            brow[0:1, off:off + w], bstage[0:1, 0:w]), waits=[e_d], inc=True)
    p.dma(bstage[0:1, 0:64], vec_d[1:2, 2048:2112], group="adb", waits=[e_bc])
    e_d = p.dma_ev("adb")
    e_s2 = p.op("vector", lambda v: v.tensor_scalar(
        brow[0:1, OFF_SK:OFF_SK + 64], bstage[0:1, 0:64], 2.0, None, ALU.mult),
        waits=[e_d], inc=True)
    e_s2 = p.op("vector", lambda v: v.tensor_scalar(
        sinkf[0:1, :], bstage[0:1, 0:64], 2.0, None, ALU.mult),
        waits=[e_s2], inc=True)
    e_skb = p.op("tensor", lambda t: t.matmul(
        psf[:, 0:64], ctx.one_row[0:1, :], brow[0:1, OFF_SK:OFF_SK + 64],
        start=True, stop=True), waits=[e_s2], inc=True)
    e_skc = p.op("vector", lambda v: v.tensor_copy(sinkc[:, :], psf[:, 0:64]),
                 waits=[e_skb], inc=True)
    for (dst, dd) in ((cos2, cos_d), (sin2, sin_d)):
        for hf in range(2):
            p.dma(ctx.wcvt_stage[:, hf * 1024:(hf + 1) * 1024],
                  dd[:, hf * 1024:(hf + 1) * 1024], group=f"adrp{hf}",
                  waits=[ctx.ulast.get(("rp", hf))] + ([e_s2] if hf == 0 else []))
            e_dd2 = p.dma_ev(f"adrp{hf}")
            ctx.ulast[("rp", hf)] = p.op("gpsimd", lambda g, dst=dst, hf=hf: g.tensor_copy(
                dst[:, hf * 1024:(hf + 1) * 1024],
                ctx.wcvt_stage[:, hf * 1024:(hf + 1) * 1024]),
                waits=[e_dd2], inc=True)
    e_rp = ctx.ulast[("rp", 1)]
    p.dma(ctx.wcvt_stage[:, 0:128], j2t_d, group="adrp2",
          waits=[ctx.ulast.get(("rp", 0))])
    e_dj = p.dma_ev("adrp2")
    e_j = p.op("gpsimd", lambda g: g.tensor_copy(j2t[:, :], ctx.wcvt_stage[:, 0:128]),
               waits=[e_dj], inc=True)
    ctx.ulast["wst"] = e_j
    p.dma(epsc[0:1, :], epscol_d, group="adb3")
    e_epl = p.dma_ev("adb3")
    p.phase()

    last = {}
    for ti in range(NT):
        e_xn = p.op("vector", lambda v, ti=ti: v.tensor_scalar(
            xb_t[:, :], ctx.resid[:, ti * D:(ti + 1) * D],
            mv[:, 2 * ti:2 * ti + 1], mv[:, 2 * ti + 1:2 * ti + 2],
            ALU.subtract, ALU.mult),
            waits=[ev_ln, last.get("xn")], inc=True)
        dst3d = ynT[:, :].rearrange("p (c t) -> p c t", t=T)[:, :, ti * 128:(ti + 1) * 128]
        last["xn"] = emit_transpose8(
            ctx, lambda ci: xb_t[:, ci * 128:(ci + 1) * 128], dst3d, wait=[e_xn])
    p.phase()

    def rope_tiles(dst, bank, e_mm, tt):
        qs = (bank // 2) * 512
        u0, u1 = (bank // 2) * 1024, (bank // 2) * 1024 + 512
        e_c = p.op("vector", lambda v, bank=bank, qs=qs: v.tensor_copy(
            qtmp[:, qs:qs + 512], psf[:, bank * 512:(bank + 1) * 512]),
            waits=[e_mm, ctx.ulast.get(("qtmp", qs))], inc=True)
        e_jm = p.op("tensor", lambda t, bank=bank, qs=qs: t.matmul(
            psf[:, (bank + 1) * 512:(bank + 2) * 512], j2t[:, :], qtmp[:, qs:qs + 512],
            start=True, stop=True),
            waits=[e_c, e_j, ctx.ulast.get(("jb", bank))], inc=True)
        e_t1 = p.op("vector", lambda v, tt=tt, qs=qs, u0=u0: v.tensor_tensor(
            um[:, u0:u0 + 512], qtmp[:, qs:qs + 512],
            cos2[:, tt * 512:(tt + 1) * 512], ALU.mult),
            waits=[e_rp, e_c, ctx.ulast.get(("um0", u0))], inc=True)
        e_t2 = p.op("vector", lambda v, tt=tt, bank=bank, u1=u1: v.tensor_tensor(
            um[:, u1:u1 + 512], psf[:, (bank + 1) * 512:(bank + 2) * 512],
            sin2[:, tt * 512:(tt + 1) * 512], ALU.mult),
            waits=[e_jm, ctx.ulast.get(("um1", u1))], inc=True)
        ctx.ulast[("jb", bank)] = e_t2
        ctx.ulast[("qtmp", qs)] = e_t2
        e_r = p.op("vector", lambda v, dst=dst, u0=u0, u1=u1: v.tensor_tensor(
            dst, um[:, u0:u0 + 512], um[:, u1:u1 + 512], ALU.add),
            waits=[e_t1, e_t2], inc=True)
        ctx.ulast[("um0", u0)] = e_r
        ctx.ulast[("um1", u1)] = e_r
        return e_r

    def proj_rope_2ch(dst, wdram, wcol0, brow_ap, e_brow, xwaits=()):
        ev_w = emit_w16_stream2(ctx, w16, wdram, wcol0, 256, "adw",
                                stage=ctx.wcvt_stage,
                                waits=[ctx.ulast.get("wst")] + list(xwaits))
        lr = {}
        e_last = None
        for col in range(2):
            for tt in range(4):
                bank = ((col * 4 + tt) % 2) * 2
                p.op("tensor", lambda t, col=col, bank=bank: t.matmul(
                    psf[:, bank * 512:(bank + 1) * 512],
                    brow_ap[0:1, col * 128:(col + 1) * 128],
                    ctx.ones_row[0:1, :], start=True, stop=False),
                    waits=[ev_w, e_brow, lr.get(bank)], inc=True)
                e_mm = None
                for ci in range(ND):
                    e_mm = p.op("tensor", lambda t, col=col, tt=tt, ci=ci, bank=bank: t.matmul(
                        psf[:, bank * 512:(bank + 1) * 512],
                        w16[:, ci * 256 + col * 128: ci * 256 + (col + 1) * 128],
                        ynT[:, ci * T + tt * 512: ci * T + (tt + 1) * 512],
                        start=False, stop=(ci == ND - 1)), inc=True)
                e_last = rope_tiles(
                    dst[:, col * T + tt * 512: col * T + (tt + 1) * 512],
                    bank, e_mm, tt)
                lr[bank] = e_last
        ctx.ulast["wst"] = e_last
        return e_last

    for qt in range(4):
        e_k = proj_rope_2ch(k_fm, wk_d, qt * 256,
                            brow[0:1, OFF_BK + qt * 256: OFF_BK + qt * 256 + 256],
                            e_bc)
        ev_wv = emit_w16_stream2(ctx, w16, wv_d, qt * 256, 256, "adw",
                                 stage=ctx.wcvt_stage, waits=[ctx.ulast["wst"]])
        e_vo = p.op("vector", lambda v: v.memset(
            v_tm[:, :].rearrange("p (ti c) -> p ti c", c=260)[:, :, 64:260:65], 1.0),
            inc=True)
        lrv = {}
        e_ev = None
        for ti in range(NT):
            bank = 4 + ti % 2
            p.op("tensor", lambda t, bank=bank, qt=qt: t.matmul(
                psf[:, bank * 512: bank * 512 + 256],
                ctx.one_row[:, :],
                brow[0:1, OFF_BV + qt * 256: OFF_BV + (qt + 1) * 256],
                start=True, stop=False),
                waits=[ev_wv, e_bc, e_vo, lrv.get(bank)], inc=True)
            e_mm = None
            for ci in range(ND):
                e_mm = p.op("tensor", lambda t, ti=ti, ci=ci, bank=bank: t.matmul(
                    psf[:, bank * 512: bank * 512 + 256],
                    ynT[:, ci * T + ti * 128: ci * T + (ti + 1) * 128],
                    w16[:, ci * 256:(ci + 1) * 256],
                    start=False, stop=(ci == ND - 1)), inc=True)
            e_ev = p.op("vector", lambda v, ti=ti, bank=bank: v.tensor_copy(
                v_tm[:, ti * 260:(ti + 1) * 260].rearrange(
                    "p (h c) -> p h c", c=65)[:, :, 0:64],
                psf[:, bank * 512: bank * 512 + 256].rearrange(
                    "p (h c) -> p h c", c=64)),
                waits=[e_mm], inc=True)
            lrv[bank] = e_ev
        ctx.ulast["wst"] = e_ev
        p.phase()

        for col in range(2):
            for ti in range(NT):
                e_tr = emit_transpose1(
                    ctx, k_fm[:, col * T + ti * 128: col * T + (ti + 1) * 128],
                    kt_t[:, :], wait=[ctx.ulast.get("ktsq")], rows=128)
                e_sqr = p.op("scalar", lambda sc: sc.activation(
                    t1_t[:, :], kt_t[:, :], AF.Square), waits=[e_tr], inc=True)
                e_r1 = p.op("vector", lambda v, col=col, ti=ti: v.reduce_sum(
                    kss[:, ti * 4 + 2 * col: ti * 4 + 2 * col + 1],
                    t1_t[:, 0:64], axis=AX), waits=[e_sqr], inc=True)
                e_r2 = p.op("vector", lambda v, col=col, ti=ti: v.reduce_sum(
                    kss[:, ti * 4 + 2 * col + 1: ti * 4 + 2 * col + 2],
                    t1_t[:, 64:128], axis=AX), waits=[e_r1], inc=True)
                ctx.ulast["ktsq"] = e_r2
        e_ks1 = p.op("vector", lambda v: v.tensor_single_scalar(
            kss[:, :], kss[:, :], 1e-6, ALU.max), waits=[ctx.ulast["ktsq"]], inc=True)
        e_ks2 = p.op("scalar", lambda sc: sc.activation(
            kss[:, :], kss[:, :], AF.Ln), waits=[e_ks1], inc=True)
        e_kss = p.op("scalar", lambda sc: sc.activation(
            kss[:, :], kss[:, :], AF.Exp, scale=-0.5,
            bias=ctx.ccol(math.log(0.125))), waits=[e_ks2], inc=True)
        for col in range(2):
            for ti in range(NT):
                e_tr = emit_transpose1(
                    ctx, k_fm[:, col * T + ti * 128: col * T + (ti + 1) * 128],
                    kt_t[:, :], wait=[ctx.ulast.get("ktsq2"), e_kss], rows=128)
                e_f1 = p.op("vector", lambda v, col=col, ti=ti: v.tensor_scalar(
                    kt_t[:, 0:64], kt_t[:, 0:64],
                    kss[:, ti * 4 + 2 * col: ti * 4 + 2 * col + 1], None, ALU.mult),
                    waits=[e_tr], inc=True)
                e_f2 = p.op("vector", lambda v, col=col, ti=ti: v.tensor_scalar(
                    kt_t[:, 64:128], kt_t[:, 64:128],
                    kss[:, ti * 4 + 2 * col + 1: ti * 4 + 2 * col + 2], None,
                    ALU.mult), waits=[e_f1], inc=True)
                ctx.ulast["ktsq2"] = emit_transpose1(
                    ctx, kt_t[:, :],
                    k_fm[:, col * T + ti * 128: col * T + (ti + 1) * 128],
                    wait=[e_f2], rows=128)
        p.phase()

        for br in range(4):
            p.dma(bstage[0:1, 0:256],
                  vec_d[0:1, br * 1024 + qt * 256: br * 1024 + (qt + 1) * 256],
                  group="adbq", waits=[ctx.ulast["wst"], ctx.ulast.get("bqrow")])
            e_bqd = p.dma_ev("adbq")
            e_bq = p.op("gpsimd", lambda g: g.tensor_copy(
                bqrow[0:1, :], bstage[0:1, 0:256]), waits=[e_bqd], inc=True)
            r0 = (br * 16 + qt * 4) * 65
            p.dma(bstage[0:1, 0:260], vne_d[0:1, r0:r0 + 260],
                  group="adbq", waits=[e_bq, ctx.ulast.get("vnq")])
            e_vnd = p.dma_ev("adbq")
            e_vn = p.op("gpsimd", lambda g: g.tensor_copy(
                vnq[0:1, :], bstage[0:1, 0:260]), waits=[e_vnd], inc=True)
            for hl in range(4):
                h_tot = br * 16 + qt * 4 + hl
                e_vn = p.op("vector", lambda v, hl=hl, h_tot=h_tot: v.scalar_tensor_tensor(
                    vnq[0:1, hl * 65:(hl + 1) * 65], vnq[0:1, hl * 65:(hl + 1) * 65],
                    sinkf[0:1, h_tot:h_tot + 1], epsc[0:1, :], ALU.mult, ALU.add),
                    waits=[e_vn, e_s2, e_epl], inc=True)
            e_bst = e_vn
            if qt == 0:
                p.dma(bstage[0:1, 0:1024],
                      vec_d[2:3, br * 1024:(br + 1) * 1024], group="adbq",
                      waits=[e_vn, ctx.ulast.get("borow")])
                e_bod = p.dma_ev("adbq")
                e_bo = p.op("gpsimd", lambda g: g.tensor_copy(
                    borow[0:1, :], bstage[0:1, 0:1024]), waits=[e_bod], inc=True)
                e_bst = e_bo
            e_q = proj_rope_2ch(q_fm, wq_d, br * 1024 + qt * 256,
                                bqrow, e_bq, xwaits=[e_bst])
            ctx.ulast["bqrow"] = e_q
            p.phase()

            for hl in range(4):
                h_tot = br * 16 + qt * 4 + hl
                b0 = (hl % 2) * 64
                col_q = hl // 2
                pend = []
                for tt in range(4):
                    cb = 4 + tt % 2
                    npairs = tt + 1
                    for k in range(npairs):
                        reg = k % 2
                        upair = ums[reg]
                        e_sg = [None, None]
                        for gp in range(2):
                            for off in range(2):
                                si = 4 * k + 2 * gp + off
                                bank = gp * 2 + off
                                e_sg[gp] = p.op("tensor", lambda t, si=si, tt=tt,
                                                b0=b0, col_q=col_q, bank=bank: t.matmul(
                                    psf[:, bank * 512:(bank + 1) * 512],
                                    k_fm[b0:b0 + 64, col_q * T + si * 128: col_q * T + (si + 1) * 128],
                                    q_fm[b0:b0 + 64, col_q * T + tt * 512: col_q * T + (tt + 1) * 512],
                                    start=True, stop=True, tile_position=(b0, 0)),
                                    waits=([ctx.ulast.get(("sb", gp))] if off == 0 else []),
                                    inc=True)
                        while sum(1 for f in pend if getattr(f, "is_ctx", False)) > 0:
                            pend.pop(0)()
                        e_z = [None, None]
                        for gp in range(2):
                            e_z[gp] = p.op("scalar", lambda sc, gp=gp: sc.activation(
                                zbs[gp], psf[:, gp * 1024:(gp + 1) * 1024], AF.Exp),
                                waits=[e_sg[gp], ctx.ulast.get(("zb", gp))], inc=True)
                            ctx.ulast[("sb", gp)] = e_z[gp]
                        e_w = [None, None]
                        for gp in range(2):
                            e_w[gp] = p.op("scalar", lambda sc, gp=gp: sc.activation(
                                zbs[gp], zbs[gp], AF.Ln, bias=ctx.ccol(1.0)),
                                waits=[e_z[gp]], inc=True)
                        e_t = [None, None]
                        for gp in range(2):
                            e_t[gp] = p.op("scalar", lambda sc, gp=gp: sc.activation(
                                tbs[gp], zbs[gp], AF.Tanh, scale=SC2),
                                waits=[e_w[gp], ctx.ulast.get(("tb", gp))], inc=True)
                        e_u = [None, None]
                        for gp in range(2):
                            e_u[gp] = p.op("vector", lambda v, gp=gp, upair=upair: v.scalar_tensor_tensor(
                                upair[:, gp * 1024:(gp + 1) * 1024], tbs[gp], 1.0,
                                zbs[gp], ALU.add, ALU.mult),
                                waits=[e_t[gp], ctx.ulast.get(("um", reg, gp))], inc=True)
                            ctx.ulast[("zb", gp)] = e_u[gp]
                            ctx.ulast[("tb", gp)] = e_u[gp]
                        e_m = p.op("vector", lambda v, h_tot=h_tot, upair=upair: v.scalar_tensor_tensor(
                            upair[:, :], upair[:, :], sinkc[:, h_tot:h_tot + 1],
                            upair[:, :], ALU.is_ge, ALU.mult),
                            waits=[e_u[0], e_u[1], e_skc], inc=True)
                        e_ms = [e_m] * 4
                        for j in range(4):
                            si = 4 * k + j
                            if si >= 4 * tt:
                                e_ms[j] = p.op("gpsimd", lambda g_, si=si, tt=tt,
                                               j=j, upair=upair: g_.affine_select(
                                    upair[:, j * 512:(j + 1) * 512],
                                    upair[:, j * 512:(j + 1) * 512],
                                    [[1, 512]], ALU.is_ge, 0.0,
                                    base=tt * 512 - si * 128, channel_multiplier=-1),
                                    waits=[e_m], inc=True)

                        def ctx_pair(k=k, reg=reg, upair=upair, cb=cb, hl=hl,
                                     tt=tt, e_ms=e_ms, last=(k == npairs - 1)):
                            e_c = None
                            for j in range(4):
                                si = 4 * k + j
                                e_c = p.op("tensor", lambda t, si=si, hl=hl, j=j,
                                           upair=upair, cb=cb: t.matmul(
                                    psf[0:65, cb * 512:(cb + 1) * 512],
                                    v_tm[:, si * 260 + hl * 65: si * 260 + (hl + 1) * 65],
                                    upair[:, j * 512:(j + 1) * 512],
                                    start=(si == 0), stop=False),
                                    waits=[e_ms[j]] + ([ctx.ulast.get(("ctxps", cb))]
                                                       if si == 0 else []),
                                    inc=True)
                            if last:
                                e_c = p.op("tensor", lambda t, hl=hl, cb=cb: t.matmul(
                                    psf[0:65, cb * 512:(cb + 1) * 512],
                                    vnq[0:1, hl * 65:(hl + 1) * 65],
                                    ctx.ones_row[0:1, :],
                                    start=False, stop=True), waits=[e_vn], inc=True)
                                ctx.ulast["vnq"] = e_c
                            ctx.ulast[("um", reg, 0)] = e_c
                            ctx.ulast[("um", reg, 1)] = e_c
                            ctx.ulast[("cstop", cb)] = e_c
                        ctx_pair.is_ctx = True
                        pend.append(ctx_pair)

                    def norm_group(tt=tt, cb=cb, hl=hl):
                        e_cp = p.op("vector", lambda v, cb=cb: v.tensor_copy(
                            ctx_sb[:, :], psf[0:65, cb * 512:(cb + 1) * 512]),
                            waits=[ctx.ulast.get(("cstop", cb)),
                                   ctx.ulast.get("ctxsb")], inc=True)
                        ctx.ulast[("ctxps", cb)] = e_cp
                        e_d3 = None
                        for ts4 in range(4):
                            ti = tt * 4 + ts4
                            e_t = emit_transpose1(ctx, ctx_sb[0:65, ts4 * 128:(ts4 + 1) * 128],
                                                  ctn[:, 0:65],
                                                  wait=[e_cp, ctx.ulast.get("um0t")],
                                                  rows=65)
                            e_d2 = p.op("vector", lambda v: v.reciprocal(
                                t1_t[:, 0:1], ctn[:, 64:65]), waits=[e_t], inc=True)
                            e_d3 = p.op("vector", lambda v, ti=ti, hl=hl: v.tensor_scalar(
                                attn_n[:, ti * 256 + hl * 64: ti * 256 + (hl + 1) * 64],
                                ctn[:, 0:64], t1_t[:, 0:1], None, ALU.mult),
                                waits=[e_d2], inc=True)
                            ctx.ulast["um0t"] = e_d3
                        ctx.ulast["ctxsb"] = e_d3
                    pend.append(norm_group)
                for f in pend:
                    f()
            p.phase()

            e_af = None
            for ti in range(NT):
                dst3d = attn_fm[:, :].rearrange("p (c t) -> p c t", t=T)[:, :, ti * 128:(ti + 1) * 128]
                e_af = emit_transpose8(
                    ctx, lambda ci, ti=ti: attn_n[:, ti * 256 + ci * 128: ti * 256 + (ci + 1) * 128],
                    dst3d, wait=[], nblk=2)
            ev_wo = emit_w16_stream(ctx, w16o, wo_d[br], 0, 1024, "adw3",
                                    waits=[ctx.ulast["wst"]], nchunks=2,
                                    row0=qt * 256)
            lro = {}
            e_last_o = None
            for ti in range(NT):
                par = ti % 2
                for oc in range(2):
                    bank = par * 2 + oc
                    if qt == 0:
                        p.op("tensor", lambda t, oc=oc, bank=bank: t.matmul(
                            psf[:, bank * 512:(bank + 1) * 512],
                            ctx.one_row[:, :],
                            borow[0:1, oc * 512:(oc + 1) * 512],
                            start=True, stop=False),
                            waits=[ev_wo, e_af, e_bo, lro.get(bank)], inc=True)
                    e_mm = None
                    for ci in range(2):
                        e_mm = p.op("tensor", lambda t, ti=ti, oc=oc, ci=ci, bank=bank,
                                    first=(qt != 0 and ci == 0): t.matmul(
                            psf[:, bank * 512:(bank + 1) * 512],
                            attn_fm[:, ci * T + ti * 128: ci * T + (ti + 1) * 128],
                            w16o[:, ci * 1024 + oc * 512: ci * 1024 + (oc + 1) * 512],
                            start=first, stop=(ci == 1)),
                            waits=([ev_wo, e_af, lro.get(bank)] if ci == 0 else []),
                            inc=True)
                    e_last_o = p.op("vector", lambda v, ti=ti, oc=oc, bank=bank: v.scalar_tensor_tensor(
                        ctx.resid[:, ti * D + oc * 512: ti * D + (oc + 1) * 512],
                        psf[:, bank * 512:(bank + 1) * 512], 0.25,
                        ctx.resid[:, ti * D + oc * 512: ti * D + (oc + 1) * 512],
                        ALU.mult, ALU.add), waits=[e_mm], inc=True)
                    lro[bank] = e_last_o
            ctx.ulast["borow"] = e_last_o
            ctx.ulast["wst"] = e_last_o
            p.phase()





def emit_transpose1_f32(ctx, src, dst, wait=(), rows=128, bank_off=3 * 512 + 256):
    p = ctx.p
    ncols = src.shape[1]
    ps = ctx.psf[0:ncols, bank_off:bank_off + rows]
    e_pe = p.op("tensor", lambda t: t.transpose(ps, src, ctx.identf[0:rows, 0:rows]),
                waits=list(wait) + [ctx.ulast.get("t1f32")], inc=True)
    e_cp = p.op("vector", lambda v: v.tensor_copy(dst, ps), waits=[e_pe], inc=True)
    ctx.ulast["t1f32"] = e_cp
    return e_cp


def _ad_vec(inputs):
    v = np.zeros((3, 4096), np.float32)
    v[0, 0:4096] = f32(inputs["L1_attn_b_b_Q"])
    v[1, 0:1024] = f32(inputs["L1_attn_b_b_K"])
    v[1, 1024:2048] = f32(inputs["L1_attn_b_b_V"])
    v[1, 2048:2112] = f32(inputs["L1_attn_b_sink_scalars"])
    v[2, 0:4096] = f32(inputs["L1_attn_b_W_O_bias"]).reshape(-1)
    return v


def _rope_table(fn):
    inv = 1.0 / 10000.0 ** (np.arange(0, 64, 2, dtype=np.float64) / 64.0)
    tt = np.arange(T, dtype=np.float64)
    fr = tt[None, :] * inv[:, None]
    tab = np.zeros((128, T), np.float64)
    for pp in range(128):
        tab[pp] = fn(fr[(pp % 64) // 2])
    return tab.astype(np.float32)


def _j2t_table():
    j64 = np.zeros((64, 64), np.float32)
    for i in range(32):
        j64[2 * i, 2 * i + 1] = -1.0
        j64[2 * i + 1, 2 * i] = 1.0
    j2 = np.zeros((128, 128), np.float32)
    j2[0:64, 0:64] = j64.T
    j2[64:128, 64:128] = j64.T
    return j2


def _ac_vec(inputs):
    v = np.zeros((10, 4096), np.float32)
    v[0, 0:1024] = f32(inputs["L0_attn_b_q_b"])
    v[0, 1024:1040] = f32(inputs["L0_attn_b_dec_b"])
    v[1, 0:1024] = f32(inputs["L0_attn_b_k_b"])
    v[1, 1024:2048] = f32(inputs["L0_attn_b_f2_b"])
    v[2, 0:1024] = f32(inputs["L0_attn_b_v_b"])
    v[3, 0:1024] = f32(inputs["L0_attn_b_o_b"])
    v[4, 0:1024] = f32(inputs["L0_attn_b_ln1_w"])
    v[5, 0:1024] = f32(inputs["L0_attn_b_ln1_b"])
    v[6, 0:1024] = f32(inputs["L0_attn_b_ln2_w"])
    v[7, 0:1024] = f32(inputs["L0_attn_b_ln2_b"])
    v[8, 0:4096] = f32(inputs["L0_attn_b_f1_b"])
    return v


def prep_host(inputs):
    H = np.array([[1.0]], dtype=np.float64)
    while H.shape[0] < D:
        H = np.block([[H, H], [H, -H]])
    H = (H / np.sqrt(D)).astype(np.float32)
    pk = np.ascontiguousarray(inputs["L1_ffn_packed_weight"]).reshape(256, 1024)
    ksmask = np.zeros((128, 128), np.float32)
    for ci in range(8):
        for pp in range(128):
            ksmask[pp, ci * 16 + 2 * ci + pp // 64] = 1.0
    arrs = {
        "aa_wqkvT": bf(f32(inputs["L0_attn_a_w_qkv"]).T.copy()),
        "aa_wprojT": bf(f32(inputs["L0_attn_a_w_proj"]).T.copy()),
        "aa_bproj": f32(inputs["L0_attn_a_b_proj"]).reshape(1, 1024),
        "c_ksmask": bf(ksmask),
        "ab_wqT": bf(f32(inputs["L1_attn_a_Wq"]).T.copy()),
        "ab_wkT": bf(f32(inputs["L1_attn_a_Wk"]).T.copy()),
        "ab_wvT": bf(f32(inputs["L1_attn_a_Wv"]).T.copy()),
        "ab_woT": bf(f32(inputs["L1_attn_a_Wo"]).T.copy()),
        "ab_bias": np.stack([f32(inputs["L1_attn_a_bq"]), f32(inputs["L1_attn_a_bk"]),
                             f32(inputs["L1_attn_a_bv"]), f32(inputs["L1_attn_a_bo"])]),
        "ab_gb": np.stack([f32(inputs["L1_attn_a_gamma"]), f32(inputs["L1_attn_a_beta"])]),
        "ac_qw": bf(f32(inputs["L0_attn_b_q_w"])),
        "ac_kw": bf(f32(inputs["L0_attn_b_k_w"])),
        "ac_vw": bf(f32(inputs["L0_attn_b_v_w"])),
        "ac_ow": bf(f32(inputs["L0_attn_b_o_w"])),
        "ac_decw": f32(inputs["L0_attn_b_dec_w"]),
        "ac_f1": bf(f32(inputs["L0_attn_b_f1_w"])),
        "ac_f2": bf(f32(inputs["L0_attn_b_f2_w"])),
        "ac_vec": _ac_vec(inputs),
        "ad_wq": bf(f32(inputs["L1_attn_b_W_Q"])),
        "ad_wk": bf(f32(inputs["L1_attn_b_W_K"])),
        "ad_wv": bf(f32(inputs["L1_attn_b_W_V"])),
        "ad_wo": bf(f32(inputs["L1_attn_b_W_O"])),
        "ad_vec": _ad_vec(inputs),
        "ad_vne": np.concatenate([
            f32(inputs["L1_attn_b_v_nulls"]).reshape(64, 64),
            np.ones((64, 1), np.float32)], axis=1).reshape(1, 64 * 65),
        "ad_cos": _rope_table(np.cos),
        "ad_sin": _rope_table(np.sin),
        "ad_j2t": _j2t_table(),
        "c_epscol": bf(np.concatenate([np.zeros(64, np.float32),
                                       np.full(1, 2e-6, np.float32)]).reshape(1, 65)),
        "hada": bf(H),
        "l0fw_T": f32(inputs["L0_ffn_weight"]).T.copy(),
        "l1f_packedT": np.ascontiguousarray(pk.T.astype(np.int32)),
        "l1f_wscale": f32(inputs["L1_ffn_weight_scale"]).reshape(1, 1),
        "l1f_bias": f32(inputs["L1_ffn_bias"]).reshape(1, 1024),
    }
    return arrs


def build_full(ins, sublayers):
    ctx = _new_ctx(ins, need_l1wq=("l1ffn" in sublayers))
    p = ctx.p
    host = prep_host(ins)
    x_dram = ctx.add_input("x_in", np.zeros((T, D), np.float32))
    y_dram = p.nc.dram_tensor("y_out", [T, D], F32, kind="ExternalOutput").ap()
    _load_resid(ctx, x_dram)
    l1ffn_seen = False
    for sl in sublayers:
        p.reset_scratch()
        p.strict = STRICT or (sl == "l0attna")
        ctx.l1wq_free = (ctx.l1_wq is not None) and not l1ffn_seen
        if sl == "l1ffn":
            l1ffn_seen = True
        if sl == "l1attnb":
            emit_l1attnb(ctx,
                         ctx.add_input("ad_wq", host["ad_wq"]),
                         ctx.add_input("ad_wk", host["ad_wk"]),
                         ctx.add_input("ad_wv", host["ad_wv"]),
                         ctx.add_input("ad_wo", host["ad_wo"]),
                         ctx.add_input("ad_vec", host["ad_vec"]),
                         ctx.add_input("ad_vne", host["ad_vne"]),
                         ctx.add_input("ad_cos", host["ad_cos"]),
                         ctx.add_input("ad_sin", host["ad_sin"]),
                         ctx.add_input("ad_j2t", host["ad_j2t"]),
                         ctx.add_input("c_epscol", host["c_epscol"]))
        elif sl == "l0attnb":
            emit_l0attnb(ctx,
                         ctx.add_input("ac_qw", host["ac_qw"]),
                         ctx.add_input("ac_kw", host["ac_kw"]),
                         ctx.add_input("ac_vw", host["ac_vw"]),
                         ctx.add_input("ac_ow", host["ac_ow"]),
                         ctx.add_input("ac_decw", host["ac_decw"]),
                         ctx.add_input("ac_f1", host["ac_f1"]),
                         ctx.add_input("ac_f2", host["ac_f2"]),
                         ctx.add_input("ac_vec", host["ac_vec"]))
        elif sl == "l1attna":
            emit_l1attna(ctx,
                         ctx.add_input("ab_wqT", host["ab_wqT"]),
                         ctx.add_input("ab_wkT", host["ab_wkT"]),
                         ctx.add_input("ab_wvT", host["ab_wvT"]),
                         ctx.add_input("ab_woT", host["ab_woT"]),
                         ctx.add_input("ab_bias", host["ab_bias"]),
                         ctx.add_input("ab_gb", host["ab_gb"]))
        elif sl == "l0attna":
            emit_l0attna(ctx,
                         ctx.add_input("aa_wqkvT", host["aa_wqkvT"]),
                         ctx.add_input("aa_wprojT", host["aa_wprojT"]),
                         ctx.add_input("aa_bproj", host["aa_bproj"]),
                         ctx.add_input("c_ksmask", host["c_ksmask"]))
        elif sl == "l0ffn":
            wd = ctx.add_input("l0fw_T", host["l0fw_T"])
            hd = ctx.add_input("hada", host["hada"])
            emit_l0ffn(ctx, wd, hd, None, None)
        elif sl == "l1ffn":
            pd = ctx.add_input("l1f_packedT", host["l1f_packedT"])
            wsd = ctx.add_input("l1f_wscale", host["l1f_wscale"])
            bd = ctx.add_input("l1f_bias", host["l1f_bias"])
            emit_l1ffn(ctx, pd, wsd, bd)
        else:
            raise ValueError(sl)
    _store_resid(ctx, y_dram)
    ctx.flush_dumps()
    nc = p.finalize()
    return ctx, nc


def run_sublayers(inputs, x_full, sublayers, n_cores=8):
    ctx, nc = build_full(inputs, sublayers)
    in_maps = []
    for c in range(n_cores):
        m = dict(ctx.in_maps)
        m["x_in"] = np.ascontiguousarray(x_full[c])
        in_maps.append(m)
    res = run_bass_kernel_spmd(nc, in_maps, core_ids=list(range(n_cores)))
    return np.stack([r["y_out"] for r in res.results])


SUBLAYER_ORDER = ["l0attna", "l0ffn", "l0attnb", "l0ffn",
                  "l1attna", "l1ffn", "l1attnb", "l1ffn"]


def kernel(**inputs):
    x = np.ascontiguousarray(np.asarray(inputs["L0_attn_a_x"], np.float32))
    return run_sublayers(inputs, x, SUBLAYER_ORDER, n_cores=8)
